# version 13
# baseline (speedup 1.0000x reference)
import numpy as np
import ml_dtypes
from contextlib import ExitStack

import concourse.bass as bass
import concourse.mybir as mybir
from concourse.bass_utils import run_bass_kernel_spmd

F32 = mybir.dt.float32
BF16 = mybir.dt.bfloat16
AX = mybir.AxisListType.X
OP = mybir.AluOpType
AF = mybir.ActivationFunctionType

D = 1024
NH = 8
HD = 256
LR = 64
I = 2816
FC = D // 128
IC = I // 128
HC = D // 128
QMAX = 127.0
MAGIC = 12582912.0
EPS = 1e-5


class Sem:
    def __init__(self, h, name):
        self.h = h
        self.name = name
        self.n = 0


class Plan:
    ENGINES = ("sync", "scalar", "vector", "tensor", "gpsimd")

    def __init__(self):
        self.ops = {e: [] for e in self.ENGINES}
        self.touched = set()

    def wait(self, eng, s, val=None):
        v = s.n if val is None else val
        assert v <= s.n, f"wait on future value of {s.name}: {v} > {s.n}"
        if v <= 0:
            return
        self.ops[eng].append(lambda e, h=s.h, v=v: e.wait_ge(h, v))

    def op(self, eng, f, inc=None, k=1):
        if inc is not None:
            inc.n += k
            self.touched.add(inc)
            self.ops[eng].append(lambda e, f=f, h=inc.h, k=k: f(e).then_inc(h, k))
        else:
            self.ops[eng].append(f)

    def dma(self, eng, out, in_, s):
        s.n += 16
        self.touched.add(s)
        self.ops[eng].append(
            lambda e, o=out, i=in_, h=s.h: e.dma_start(out=o, in_=i).then_inc(h, 16))

    def barrier(self, engines=None):
        sems = [(s, s.n) for s in self.touched]
        for eng in (engines or self.ENGINES):
            for s, v in sems:
                if v > 0:
                    self.ops[eng].append(lambda e, h=s.h, v=v: e.wait_ge(h, v))


class Slots:

    def __init__(self, plan, n):
        self.P = plan
        self.last = [[] for _ in range(n)]

    def acquire(self, eng, idx=0):
        for s, v in self.last[idx]:
            self.P.wait(eng, s, v)
        self.last[idx] = []

    def consumed(self, idx, sem, val=None):
        self.last[idx].append((sem, sem.n if val is None else val))


def build_nc(T=4096, taps=(), upto='all', flags=()):
    TT = T // 512
    KCN = T // 128
    nc = bass.Bass("TRN2", target_bir_lowering=False,
                   detect_race_conditions=False)

    xT = nc.dram_tensor("xT", [D, T], F32, kind="ExternalInput")
    wqT = nc.dram_tensor("wqT", [D, NH * HD], F32, kind="ExternalInput")
    wkvT = nc.dram_tensor("wkvT", [D, LR], F32, kind="ExternalInput")
    wkuT = nc.dram_tensor("wkuT", [LR, NH * HD], F32, kind="ExternalInput")
    wvuT = nc.dram_tensor("wvuT", [LR, NH * HD], F32, kind="ExternalInput")
    woT = nc.dram_tensor("woT", [NH * HD, D], F32, kind="ExternalInput")
    wg_rm = nc.dram_tensor("wg_rm", [I, D], F32, kind="ExternalInput")
    wu_rm = nc.dram_tensor("wu_rm", [I, D], F32, kind="ExternalInput")
    wd_rm = nc.dram_tensor("wd_rm", [D, I], F32, kind="ExternalInput")
    ident = nc.dram_tensor("ident", [128, 128], BF16, kind="ExternalInput")
    clipx = nc.dram_tensor("clipx", [1, 1], F32, kind="ExternalInput")
    clipd = nc.dram_tensor("clipd", [1, 1], F32, kind="ExternalInput")
    out_h = nc.dram_tensor("out", [D, T], F32, kind="ExternalOutput")

    r1h = nc.dram_tensor("r1h", [D, T], F32)
    r2h = nc.dram_tensor("r2h", [D, T], F32)
    hh = nc.dram_tensor("hh", [IC, TT, 128, 512], BF16)
    oh = nc.dram_tensor("oh", [16, TT, 128, 512], BF16)
    wgc_h = nc.dram_tensor("wgc_h", [FC, 128, I], BF16)
    wuc_h = nc.dram_tensor("wuc_h", [FC, 128, I], BF16)
    wdc_h = nc.dram_tensor("wdc_h", [IC, 128, D], BF16)
    rcph = nc.dram_tensor("rcph", [I], F32)
    smaxh = nc.dram_tensor("smaxh", [128], F32)
    statsh1 = nc.dram_tensor("statsh1", [TT, 1024], F32)
    statsh3 = nc.dram_tensor("statsh3", [TT, 1024], F32)
    xn2h = nc.dram_tensor("xn2h", [FC, 128, T], BF16)
    cc_in = [nc.dram_tensor(f"cc{k}_in", [1, 16], F32) for k in range(4)]
    cc_out = [nc.dram_tensor(f"cc{k}_out", [1, 16], F32) for k in range(4)]
    tapd = {t: nc.dram_tensor(t, [D, T], F32, kind="ExternalOutput")
            for t in taps if t.startswith("tap_")}
    dbg_h = (nc.dram_tensor("dbg", [128, 64], F32, kind="ExternalOutput")
             if "dbg" in taps else None)

    ctx = ExitStack()
    e_ = ctx.enter_context

    ARENA_ELEMS = 76800
    arena = e_(nc.sbuf_tensor("arena", [128, ARENA_ELEMS], BF16))

    def _shp(ap, shape):
        if len(shape) == 1:
            return ap
        pat = ("p (" + " ".join(f"d{i}" for i in range(len(shape))) + ") -> p "
               + " ".join(f"d{i}" for i in range(len(shape))))
        return ap.rearrange(pat, **{f"d{i}": shape[i] for i in range(len(shape))})

    def arb(off, *shape):
        n = int(np.prod(shape))
        return _shp(arena[:, off:off + n], shape)

    def arf(off, *shape):
        n = int(np.prod(shape))
        return _shp(arena[:, off:off + 2 * n].bitcast(F32), shape)

    wgc = arb(0, FC, I)
    wuc = arb(FC * I, FC, I)
    wdc = arb(0, IC, D)
    hload = arb(IC * D, 2, IC, 512)
    hq = hload
    xn2 = arb(0, FC, T)
    wqb = arb(8 * T, FC, NH * HD)
    qb = arb(49152, 2, T)
    kb = arb(57344, 2, T)
    vb = arb(65536, KCN, HD)
    wob = arb(0, 16, D)
    ot = arb(16384, 2, 16, 512)
    wst4 = arf(49152, 2, I)
    wsto = arf(61440, 2, D)
    rt = arf(45056, 2, FC, 512)
    qx = arb(61440, 2, FC, 512)
    gate_s = arf(69632, 4, 512)
    htile = arb(73728, 4, 512)
    wsm = [arf(61440 + k * 2048, 1024) for k in range(4)]
    wbg = [arf(61440, 2816), arf(67072, 2816)]

    sq = e_(nc.sbuf_tensor("sq", [128, 4, 512], BF16))
    rtb = e_(nc.sbuf_tensor("rtb", [128, 4, 512], BF16))
    u_t = e_(nc.sbuf_tensor("u_t", [128, 4, 512], F32))
    srow = e_(nc.sbuf_tensor("srow", [1, 2048], F32))
    cb = e_(nc.sbuf_tensor("cb", [64, TT, 512], BF16))
    wkvb = e_(nc.sbuf_tensor("wkvb", [128, FC, LR], BF16))
    wkub = e_(nc.sbuf_tensor("wkub", [64, NH * HD], BF16))
    wvub = e_(nc.sbuf_tensor("wvub", [64, NH * HD], BF16))
    wsq = e_(nc.sbuf_tensor("wsq", [64, NH * HD], BF16))
    pt = e_(nc.sbuf_tensor("pt", [128, 4, 512], BF16))
    dn = e_(nc.sbuf_tensor("dn", [128, 512], BF16))
    rdnB = e_(nc.sbuf_tensor("rdnB", [128, 512], F32))
    obuf = e_(nc.sbuf_tensor("obuf", [128, 2, 2, 512], BF16))
    rs_g = e_(nc.sbuf_tensor("rs_g", [128, IC], F32))
    rs_u = e_(nc.sbuf_tensor("rs_u", [128, IC], F32))
    rs_d = e_(nc.sbuf_tensor("rs_d", [128, HC], F32))
    rs_gs = e_(nc.sbuf_tensor("rs_gs", [128, IC], F32))
    rs_us = e_(nc.sbuf_tensor("rs_us", [128, IC], F32))
    rs_ds = e_(nc.sbuf_tensor("rs_ds", [128, HC], F32))
    smax = e_(nc.sbuf_tensor("smax", [128, 24], F32))
    svec = e_(nc.sbuf_tensor("svec", [128, 8], F32))
    sc1 = e_(nc.sbuf_tensor("sc1", [1, 32], F32))
    ccst = e_(nc.sbuf_tensor("ccst", [1, 16], F32))
    ones_b = e_(nc.sbuf_tensor("ones_b", [128, 2], BF16))
    cvec = e_(nc.sbuf_tensor("cvec", [128, 3], F32))
    nc.const_aps.register(F32, float(MAGIC), cvec[:, 1:2])
    nc.const_aps.register(F32, float(-MAGIC), cvec[:, 2:3])
    ones_r = e_(nc.sbuf_tensor("ones_r", [1, 128], F32))
    idb = e_(nc.sbuf_tensor("idb", [128, 128], BF16))
    p0s = e_(nc.sbuf_tensor("p0s", [128, 4], F32))
    ones64 = e_(nc.sbuf_tensor("ones64", [64, 1], BF16))

    pb = [e_(nc.psum_tensor(f"pb{i}", [128, 512], F32)) for i in range(8)]

    P = Plan()

    def S_(name):
        return Sem(e_(nc.semaphore(name)), name)

    s_ld = S_("ld")
    s_hq = S_("hq")
    s_ld2 = S_("ld2")
    s_st = S_("st")

    s_mm = S_("mm")
    s_ev = S_("ev")
    s_vq = S_("vq")
    s_sc = S_("sc")
    s_row = S_("row")
    s_cc = S_("cc")
    s_stat = S_("stat")
    s_bc = S_("bc")
    s_out = S_("out")
    d_ws = [S_(f"dws{k}") for k in range(4)]
    d_wb = [S_("dwb0"), S_("dwb1")]
    d_rt = [S_("drt0"), S_("drt1")]
    d_rts = [S_("drts0"), S_("drts1")]
    d_hl = [S_("dhl0"), S_("dhl1")]
    d_ht = [S_(f"dht{i}") for i in range(4)]
    d_ob = [S_("dob0"), S_("dob1")]
    d_ot = [S_("dot0"), S_("dot1")]
    d_w4 = [S_("dw40"), S_("dw41")]
    d_wo = [S_("dwo0"), S_("dwo1")]
    d_cc = S_("dcc")
    d_ccs = S_("dccs")
    d_sr = S_("dsr")
    d_clip = S_("dclip")

    blk = e_(nc.Block())

    def finish():
        if dbg_h is not None:
            P.wait("sync", s_vq)
            P.dma("sync", dbg_h[0:1, 0:32], sc1[:, :], s_st)
            P.dma("sync", dbg_h[:, 32:40], smax[:, 0:8], s_st)
            P.dma("sync", dbg_h[:, 40:44], svec[:, 0:4], s_st)
            P.dma("sync", dbg_h[:, 48:56], rs_gs[:, 0:8], s_st)
            P.dma("sync", dbg_h[:, 56:64], rs_ds[:, 0:8], s_st)
        P.wait("sync", d_rts[0])
        P.wait("sync", d_rts[1])
        P.wait("sync", s_st)

    def _replay():
        @blk.sync
        def _(eng):
            for f in P.ops["sync"]:
                f(eng)

        @blk.scalar
        def _(eng):
            for f in P.ops["scalar"]:
                f(eng)

        @blk.vector
        def _(eng):
            for f in P.ops["vector"]:
                f(eng)

        @blk.tensor
        def _(eng):
            for f in P.ops["tensor"]:
                f(eng)

        @blk.gpsimd
        def _(eng):
            for f in P.ops["gpsimd"]:
                f(eng)

        ctx.close()
        return nc

    def vfence():
        P.wait("vector", s_vq)

    def mm(psum_ap, lhsT, rhs, start, stop, inc=None, k=1):
        P.op("tensor",
             lambda e, o=psum_ap, l=lhsT, r=rhs, a=start, b=stop: e.matmul(
                 o, l, r, start=a, stop=b), inc=inc, k=k)

    sl_rt = Slots(P, 2)
    sl_qx = Slots(P, 2)
    sl_hload = Slots(P, 2)
    sl_gate = Slots(P, 4)
    sl_htile = Slots(P, 4)
    sl_ut = Slots(P, 4)
    sl_sq = Slots(P, 4)
    sl_rtb = Slots(P, 4)
    sl_wst = Slots(P, 2)
    sl_pt = Slots(P, 4)
    sl_ot = Slots(P, 2)
    sl_obuf = Slots(P, 2)
    sl_bank = Slots(P, 8)
    sl_stage = Slots(P, 1)
    sl_srow = Slots(P, 1)
    sl_qb = Slots(P, 1)
    sl_kb = Slots(P, 1)
    sl_vb = Slots(P, 1)

    P.op("vector", lambda e: e.memset(ones_b[:, 0:1], 1.0))
    P.op("vector", lambda e: e.memset(ones_b[:, 1:2], 1.0 / D))
    P.op("vector", lambda e: e.memset(ones_r[:, :], 1.0))
    P.op("vector", lambda e: e.memset(ones64[:, :], 1.0))
    P.op("vector", lambda e: e.memset(smax[:, :], 0.0))
    P.op("vector", lambda e: e.memset(cvec[:, 0:1], EPS))
    P.op("vector", lambda e: e.memset(cvec[:, 1:2], MAGIC))
    P.op("vector", lambda e: e.memset(cvec[:, 2:3], -MAGIC))
    P.op("vector", lambda e: e.memset(sc1[:, :], 0.0), inc=s_vq)

    P.wait("sync", s_vq)
    P.wait("tensor", s_vq)
    P.dma("sync", sc1[0:1, 2:3], clipx[:, :], d_clip)
    P.dma("sync", sc1[0:1, 3:4], clipd[:, :], d_clip)
    P.wait("scalar", d_clip)
    P.wait("scalar", s_vq)
    P.op("scalar", lambda e: e.activation(sc1[0:1, 0:1], sc1[0:1, 2:3], AF.Sigmoid))
    P.op("scalar", lambda e: e.activation(sc1[0:1, 1:2], sc1[0:1, 3:4], AF.Sigmoid),
         inc=s_sc)
    sig_done = (s_sc, s_sc.n)

    sl_wsm = Slots(P, 4)
    sl_wbg = Slots(P, 2)
    sl_p0s = Slots(P, 2)
    WQ = ("gpsimd", "sync")
    cnt = {"pt": 0, "bk": 0, "ob": 0, "bgfirst": True}
    pend_evac = [None]

    def run_evac():
        if pend_evac[0] is not None:
            f = pend_evac[0]
            pend_evac[0] = None
            f()

    def em_ident():
        P.dma("sync", idb[:, :], ident[:, :], s_ld2)
        P.wait("tensor", s_ld2)

    def em_tchunk(w_rm, rs_t, c, ncol, dst3, big):
        nj = (ncol + 511) // 512
        if big:
            k = c % 2
            wap, dsem, sl = wbg[k], d_wb[k], sl_wbg
        else:
            k = c % 4
            wap, dsem, sl = wsm[k], d_ws[k], sl_wsm
        q = WQ[k % 2]
        sl.acquire(q, k)
        if big and cnt["bgfirst"]:
            P.wait(q, s_sc)
            if k == 1:
                cnt["bgfirst"] = False
        P.dma(q, wap[:, 0:ncol], w_rm[c * 128:(c + 1) * 128, :], dsem)
        P.wait("vector", dsem)
        P.op("vector", lambda e, c=c, r=rs_t, n=ncol, w=wap: e.reduce_max(
            r[:, c:c + 1], w[:, 0:n], AX,
            apply_absolute_value=True), inc=s_vq)
        vfence()
        pp = c % 2
        sl_p0s.acquire("vector", pp)
        P.op("vector", lambda e, c=c, r=rs_t, pp=pp: e.tensor_scalar(
            p0s[:, 2 * pp:2 * pp + 1], r[:, c:c + 1], 1.0 / QMAX, None,
            OP.mult), inc=s_vq)
        vfence()
        P.op("vector", lambda e, pp=pp: e.reciprocal(
            p0s[:, 2 * pp + 1:2 * pp + 2], p0s[:, 2 * pp:2 * pp + 1]),
            inc=s_vq)
        P.wait("scalar", s_vq)
        for jn in range(nj):
            w = min(512, ncol - jn * 512)
            us = jn % 2
            sl_ut.acquire("scalar", us)
            P.op("scalar", lambda e, us=us, w=w, jn=jn, wap=wap, pp=pp:
                 e.activation(u_t[:, us, 0:w],
                              wap[:, jn * 512:jn * 512 + w], AF.Copy,
                              scale=p0s[:, 2 * pp + 1:2 * pp + 2],
                              bias=float(MAGIC)))
            ps = cnt["pt"] % 4
            cnt["pt"] += 1
            sl_pt.acquire("scalar", ps)
            P.op("scalar", lambda e, us=us, w=w, ps=ps: e.activation(
                pt[:, ps, 0:w], u_t[:, us, 0:w], AF.Copy,
                bias=float(-MAGIC)), inc=s_sc)
            sl_ut.consumed(us, s_sc)
            if jn == nj - 1:
                sl.consumed(k, s_sc)
                sl_p0s.consumed(pp, s_sc)
            nb = (w + 127) // 128
            b = cnt["bk"] % 4
            cnt["bk"] += 1
            sl_bank.acquire("tensor", b)
            P.wait("tensor", s_sc)
            for kk in range(nb):
                mm(pb[b][:, kk * 128:(kk + 1) * 128],
                   pt[:, ps, kk * 128:(kk + 1) * 128], idb[:, :],
                   True, True, inc=s_mm if kk == nb - 1 else None)
            sl_pt.consumed(ps, s_mm)
            mmv = s_mm.n
            run_evac()

            def evac(jn=jn, nb=nb, c=c, b=b, mmv=mmv, dst3=dst3):
                P.wait("vector", s_mm, mmv)
                if dst3 is not None:
                    P.op("vector", lambda e, d=dst3, jn=jn, nb=nb, c=c, b=b:
                         e.tensor_copy(
                             d[:, jn * 4:jn * 4 + nb, c * 128:(c + 1) * 128],
                             pb[b][:, 0:nb * 128]), inc=s_ev)
                    sl_bank.consumed(b, s_ev)
                else:
                    ob = cnt["ob"] % 2
                    cnt["ob"] += 1
                    sl_obuf.acquire("vector", ob)
                    P.op("vector", lambda e, ob=ob, nb=nb, b=b: e.tensor_copy(
                        obuf[:, ob, 0, 0:nb * 128], pb[b][:, 0:nb * 128]),
                        inc=s_ev)
                    sl_bank.consumed(b, s_ev)
                    P.wait("sync", s_ev)
                    for kk in range(nb):
                        P.dma("sync",
                              wdc_h[jn * 4 + kk][:, c * 128:(c + 1) * 128],
                              obuf[:, ob, 0, kk * 128:(kk + 1) * 128],
                              d_ob[ob])
                    sl_obuf.consumed(ob, d_ob[ob])
            pend_evac[0] = evac

    def em_spill_codes(dst3, spill_h):
        run_evac()
        P.wait("sync", s_ev)
        for cd in range(FC):
            P.dma("sync", spill_h[cd], dst3[:, cd, :], s_st)

    p0_list = [em_ident]
    for c in range(IC):
        p0_list.append(lambda c=c: em_tchunk(wg_rm, rs_g, c, D, wgc, False))
    p0_list.append(lambda: em_spill_codes(wgc, wgc_h))
    for c in range(IC):
        p0_list.append(lambda c=c: em_tchunk(wu_rm, rs_u, c, D, wuc, False))
    p0_list.append(lambda: em_spill_codes(wuc, wuc_h))
    for c in range(HC):
        p0_list.append(lambda c=c: em_tchunk(wd_rm, rs_d, c, I, None, True))
    p0_list.append(run_evac)

    def ln_stats_p1(src_tile, j):
        sl_bank.acquire("tensor", 6)
        sl_bank.acquire("tensor", 7)
        for g in range(FC // 4):
            for c4 in range(4):
                c = g * 4 + c4
                sl_sq.acquire("scalar", c4)
                P.op("scalar", lambda e, c=c, c4=c4, s=src_tile: e.activation(
                    sq[:, c4, :], s[:, c, :], AF.Square), inc=s_sc)
                sl_rtb.acquire("scalar", c4)
                P.op("scalar", lambda e, c=c, c4=c4, s=src_tile: e.copy(
                    rtb[:, c4, :], s[:, c, :]), inc=s_sc)
            P.wait("tensor", s_sc)
            for c4 in range(4):
                c = g * 4 + c4
                mm(pb[6][0:1, :], ones_b[:, 1:2], rtb[:, c4, :],
                   c == 0, c == FC - 1)
                mm(pb[7][0:1, :], ones_b[:, 1:2], sq[:, c4, :],
                   c == 0, c == FC - 1, inc=s_stat)
            for c4 in range(4):
                sl_sq.consumed(c4, s_stat)
                sl_rtb.consumed(c4, s_stat)

    def ln_stats_p2(j, stats_dst):
        sl_srow.acquire("scalar", 0)
        P.wait("scalar", s_stat)
        P.op("scalar", lambda e: e.activation(srow[0:1, 0:512], pb[6][0:1, :],
                                              AF.Square), inc=s_sc)
        P.wait("vector", s_sc)
        P.op("vector", lambda e: e.tensor_sub(srow[0:1, 512:1024], pb[7][0:1, :],
                                              srow[0:1, 0:512]), inc=s_vq)
        P.wait("scalar", s_vq)
        P.op("scalar", lambda e: e.activation(srow[0:1, 0:512],
                                              srow[0:1, 512:1024], AF.Sqrt,
                                              bias=cvec[0:1, 0:1]), inc=s_sc)
        P.wait("vector", s_sc)
        P.op("vector", lambda e: e.reciprocal(
            srow[0:1, 1024:1536], srow[0:1, 0:512]))
        P.op("vector", lambda e: e.tensor_mul(
            srow[0:1, 1536:2048], pb[6][0:1, :], srow[0:1, 1024:1536]),
            inc=s_row)
        sl_bank.consumed(6, s_row)
        sl_bank.consumed(7, s_row)
        P.wait("sync", s_row)
        P.dma("sync", stats_dst[j], srow[0:1, 1024:2048], d_sr)
        sl_srow.consumed(0, d_sr)

    def ln_stats_tile(src_tile, j, stats_dst):
        ln_stats_p1(src_tile, j)
        ln_stats_p2(j, stats_dst)

    def restore_stats(stats_src, j, scale_slot):
        sl_srow.acquire("sync", 0)
        P.dma("sync", srow[0:1, 1024:2048], stats_src[j], s_ld2)
        P.wait("vector", s_ld2)
        P.op("vector", lambda e, ss=scale_slot: e.tensor_scalar(
            srow[0:1, 0:512], srow[0:1, 1024:1536], sc1[0:1, ss:ss + 1],
            None, OP.mult))
        P.op("vector", lambda e, ss=scale_slot: e.tensor_scalar(
            srow[0:1, 512:1024], srow[0:1, 1536:2048], sc1[0:1, ss:ss + 1],
            None, OP.mult), inc=s_row)

    def bcast_pair(row_a, row_b, ia, ib, gate_sem):
        sl_bank.acquire("tensor", ia)
        sl_bank.acquire("tensor", ib)
        P.wait("tensor", gate_sem)
        mm(pb[ia][:, :], ones_r[:, :], row_a, True, True)
        mm(pb[ib][:, :], ones_r[:, :], row_b, True, True, inc=s_bc)
        sl_srow.consumed(0, s_bc)

    def cross_part_max(src_col, dst_slot):
        P.wait("sync", s_vq)
        P.wait("sync", s_ev)
        sl_srow.acquire("sync", 0)
        P.dma("sync", smaxh[0:128], src_col, s_st)
        P.wait("sync", s_st)
        P.dma("sync", srow[0:1, 0:128], smaxh[0:128], s_ld2)
        sl_srow.consumed(0, s_vq, s_vq.n + 1)
        P.wait("vector", s_ld2)
        P.op("vector", lambda e, d=dst_slot: e.reduce_max(
            sc1[0:1, d:d + 1], srow[0:1, 0:128], AX), inc=s_vq)
        vfence()

    def collective_max(idx, src_slot, dst_slot):
        for k in range(16):
            P.op("vector", lambda e, k=k, s=src_slot: e.tensor_copy(
                ccst[0:1, k:k + 1], sc1[0:1, s:s + 1]),
                inc=s_vq if k == 15 else None)
        P.wait("sync", s_vq)
        P.dma("sync", cc_in[idx][:, :], ccst[:, :], d_ccs)
        P.wait("gpsimd", d_ccs)
        P.op("gpsimd", lambda e, idx=idx: e.collective_compute(
            "AllReduce", OP.max, replica_groups=[list(range(8))],
            ins=[cc_in[idx].ap().opt()], outs=[cc_out[idx].ap().opt()]),
            inc=s_cc)
        P.wait("gpsimd", s_cc)
        P.dma("gpsimd", ccst[:, :], cc_out[idx][:, :], d_cc)
        P.wait("vector", d_cc)
        P.op("vector", lambda e, d=dst_slot: e.tensor_copy(
            sc1[0:1, d:d + 1], ccst[0:1, 8:9]), inc=s_vq)
        vfence()

    def make_scale(gmax_slot, sig_slot, scale_slot, inv_slot, svec_off):
        P.wait("vector", sig_done[0], sig_done[1])
        P.op("vector", lambda e, g=gmax_slot, s=sig_slot, o=scale_slot:
             e.tensor_scalar(sc1[0:1, o:o + 1], sc1[0:1, g:g + 1],
                             sc1[0:1, s:s + 1], 1.0 / QMAX, OP.mult, OP.mult),
             inc=s_vq)
        vfence()
        P.op("vector", lambda e, o=scale_slot, v=inv_slot: e.reciprocal(
            sc1[0:1, v:v + 1], sc1[0:1, o:o + 1]), inc=s_vq)
        vfence()
        sl_bank.acquire("tensor", 4)
        P.wait("tensor", s_vq)
        mm(pb[4][:, svec_off:svec_off + 1], ones_r[:, :],
           sc1[0:1, scale_slot:scale_slot + 1], True, True)
        mm(pb[4][:, svec_off + 1:svec_off + 2], ones_r[:, :],
           sc1[0:1, inv_slot:inv_slot + 1], True, True, inc=s_bc)
        P.wait("scalar", s_bc)
        P.op("scalar", lambda e, o=svec_off: e.copy(
            svec[:, o:o + 2], pb[4][:, o:o + 2]), inc=s_sc)
        sl_bank.consumed(4, s_sc)

    def load_rtile(src_hbm, j, buf):
        sl_rt.acquire("gpsimd", buf)
        for c in range(FC):
            P.dma("gpsimd", rt[:, buf, c, :],
                  src_hbm[c * 128:(c + 1) * 128, j * 512:(j + 1) * 512],
                  d_rt[buf])

    def em_pa_A(src_hbm, j):
        buf = j % 2
        load_rtile(src_hbm, j, buf)
        P.wait("scalar", d_rt[buf])
        ln_stats_p1(rt[:, buf], j)

    def em_pa_B(stats_dst, j):
        ln_stats_p2(j, stats_dst)

    def em_pa_C(j):
        buf = j % 2
        bcast_pair(srow[0:1, 1024:1536], srow[0:1, 1536:2048], 4, 5, s_row)
        P.wait("vector", s_bc)
        P.wait("vector", d_rt[buf])
        for c in range(FC):
            cs = 2 + (c % 2)
            sl_ut.acquire("vector", cs)
            P.op("vector", lambda e, c=c, buf=buf, cs=cs: e.tensor_mul(
                u_t[:, cs, :], rt[:, buf, c, :], pb[4][:, :]))
            P.op("vector", lambda e, cs=cs: e.tensor_sub(
                u_t[:, cs, :], u_t[:, cs, :], pb[5][:, :]))
            P.op("vector", lambda e, c=c, cs=cs: e.reduce_max(
                smax[:, 8 + c:9 + c], u_t[:, cs, :], AX,
                apply_absolute_value=True), inc=s_vq)
            sl_ut.consumed(cs, s_vq)
        vfence()
        P.op("vector", lambda e, j=j: e.reduce_max(
            smax[:, 16 + j:17 + j], smax[:, 8:16], AX), inc=s_vq)
        sl_rt.consumed(buf, s_vq)
        sl_bank.consumed(4, s_vq)
        sl_bank.consumed(5, s_vq)

    def run_front():
        pa_list = [lambda: P.op("vector",
                                lambda e: e.memset(smax[:, 0:1], 0.0),
                                inc=s_vq)]
        for j in range(TT):
            pa_list.append(lambda j=j: em_pa_A(xT, j))
            pa_list.append(lambda j=j: em_pa_B(statsh1, j))
            pa_list.append(lambda j=j: em_pa_C(j))

        def pa_final():
            vfence()
            P.op("vector", lambda e: e.reduce_max(
                smax[:, 0:1], smax[:, 16:16 + TT], AX), inc=s_vq)
        pa_list.append(pa_final)
        n0, n1 = len(p0_list), len(pa_list)
        k = 0
        for i, f in enumerate(p0_list):
            f()
            if k < n1 and (i + 1) * (n1 + 1) >= (k + 1) * n0:
                pa_list[k]()
                k += 1
        while k < n1:
            pa_list[k]()
            k += 1

    def ffn_passB(src_hbm, stats_src, cc_idx):
        P.wait("tensor", s_ld)
        cross_part_max(smax[:, 0:1], 4)
        collective_max(cc_idx, 4, 5)
        make_scale(5, 0, 6, 7, 0)
        P.wait("vector", s_sc)
        P.op("vector", lambda e: e.tensor_scalar(rs_gs[:, 0:IC], rs_g[:, 0:IC],
                                                 svec[:, 0:1], 1.0 / QMAX,
                                                 OP.mult, OP.mult))
        P.op("vector", lambda e: e.tensor_scalar(rs_us[:, 0:IC], rs_u[:, 0:IC],
                                                 svec[:, 0:1], 1.0 / QMAX,
                                                 OP.mult, OP.mult), inc=s_vq)
        P.op("vector", lambda e: e.memset(smax[:, 2:3], 0.0), inc=s_vq)
        P.barrier()
        for j in range(TT):
            buf = j % 2
            load_rtile(src_hbm, j, buf)
            restore_stats(stats_src, j, scale_slot=7)
            bcast_pair(srow[0:1, 0:512], srow[0:1, 512:1024], 4, 5, s_row)
            P.wait("vector", s_bc)
            P.wait("vector", d_rt[buf])
            sl_qx.acquire("vector", buf)
            for c in range(FC):
                cs = c % 2
                sl_ut.acquire("vector", cs)
                sl_ut.acquire("vector", 2 + cs)
                P.op("vector", lambda e, c=c, buf=buf, cs=cs: e.tensor_mul(
                    u_t[:, cs, :], rt[:, buf, c, :], pb[4][:, :]))
                P.op("vector", lambda e, cs=cs: e.tensor_sub(
                    u_t[:, 2 + cs, :], u_t[:, cs, :], pb[5][:, :]))
                P.op("vector", lambda e, cs=cs: e.tensor_scalar(
                    u_t[:, cs, :], u_t[:, 2 + cs, :], MAGIC, MAGIC - 128.0,
                    OP.add, OP.max))
                P.op("vector", lambda e, c=c, buf=buf, cs=cs: e.tensor_scalar(
                    qx[:, buf, c, :], u_t[:, cs, :], MAGIC + 127.0, MAGIC,
                    OP.min, OP.subtract), inc=s_vq)
                sl_ut.consumed(cs, s_vq)
                sl_ut.consumed(2 + cs, s_vq)
            sl_rt.consumed(buf, s_vq)
            sl_bank.consumed(4, s_vq)
            sl_bank.consumed(5, s_vq)
            P.wait("tensor", s_vq)
            for i in range(IC):
                for half, wc in enumerate((wgc, wuc)):
                    bslot = (2 * i + half) % 4
                    sl_bank.acquire("tensor", bslot)
                    for c in range(FC):
                        mm(pb[bslot][:, :], wc[:, c, i * 128:(i + 1) * 128],
                           qx[:, buf, c, :], c == 0, c == FC - 1,
                           inc=s_mm if c == FC - 1 else None)
                gate_done = s_mm.n - 1
                up_done = s_mm.n
                gslot = (2 * i) % 4
                uslot = (2 * i + 1) % 4
                hslot = i % 4
                sl_gate.acquire("scalar", hslot)
                P.wait("scalar", s_mm, gate_done)
                P.op("scalar", lambda e, i=i, g=gslot, hs=hslot: e.activation(
                    gate_s[:, hs, :], pb[g][:, :], AF.Silu,
                    scale=rs_gs[:, i:i + 1]), inc=s_sc)
                sl_bank.consumed(gslot, s_sc)
                sl_htile.acquire("vector", hslot)
                P.wait("vector", s_mm, up_done)
                P.wait("vector", s_sc)
                P.op("vector", lambda e, i=i, u=uslot, hs=hslot:
                     e.scalar_tensor_tensor(
                         htile[:, hs, :], pb[u][:, :], rs_us[:, i:i + 1],
                         gate_s[:, hs, :], OP.mult, OP.mult))
                P.op("vector", lambda e, hs=hslot, i=i: e.reduce_max(
                    smax[:, 8 + i % 8:9 + i % 8], htile[:, hs, :], AX,
                    apply_absolute_value=True), inc=s_ev)
                sl_bank.consumed(uslot, s_ev)
                sl_gate.consumed(hslot, s_ev)
                if i % 8 == 7 or i == IC - 1:
                    g = (i % 8) + 1
                    vfence()
                    P.wait("vector", s_ev)
                    P.op("vector", lambda e, g=g: e.reduce_max(
                        smax[:, 3:4], smax[:, 8:8 + g], AX), inc=s_vq)
                    vfence()
                    P.op("vector", lambda e: e.tensor_max(
                        smax[:, 2:3], smax[:, 2:3], smax[:, 3:4]), inc=s_vq)
                    vfence()
                P.wait("sync", s_ev)
                P.dma("sync", hh[i, j], htile[:, hslot, :], d_ht[hslot])
                sl_htile.consumed(hslot, d_ht[hslot])
            sl_qx.consumed(buf, s_mm)

    def ffn_passC(src_hbm, dst_hbm, cc_idx, store_sem, tap=None,
                  fuse_ln2=False):
        P.wait("gpsimd", s_mm)
        P.wait("gpsimd", s_st)
        P.wait("gpsimd", d_ob[0])
        P.wait("gpsimd", d_ob[1])
        for c in range(IC):
            P.dma("gpsimd", wdc[:, c, :], wdc_h[c], s_ld)
        cross_part_max(smax[:, 2:3], 8)
        collective_max(cc_idx, 8, 9)
        make_scale(9, 1, 10, 11, 2)
        P.wait("vector", s_sc)
        P.op("vector", lambda e: e.tensor_scalar(
            rs_ds[:, 0:HC], rs_d[:, 0:HC], svec[:, 2:3], 0.5 / QMAX,
            OP.mult, OP.mult), inc=s_vq)
        P.barrier()
        pending = [None]

        def run_pending():
            if pending[0] is not None:
                f = pending[0]
                pending[0] = None
                f()

        for j in range(TT):
            buf = j % 2
            sl_hload.acquire("gpsimd", buf)
            for i in range(IC):
                P.dma("gpsimd", hload[:, buf, i, :], hh[i, j], d_hl[buf])
            load_rtile(src_hbm, j, buf)
            P.wait("scalar", d_hl[buf])
            for i in range(IC):
                cs = i % 2
                sl_ut.acquire("scalar", cs)
                P.op("scalar", lambda e, i=i, buf=buf, cs=cs: e.activation(
                    u_t[:, cs, :], hload[:, buf, i, :], AF.Copy,
                    bias=float(MAGIC), scale=svec[:, 3:4]), inc=s_sc)
                sl_ut.acquire("scalar", 2 + cs)
                P.op("scalar", lambda e, cs=cs: e.activation(
                    u_t[:, 2 + cs, :], u_t[:, cs, :], AF.Copy,
                    bias=float(-MAGIC)), inc=s_sc)
                sl_ut.consumed(cs, s_sc)
                P.wait("vector", s_sc)
                P.op("vector", lambda e, i=i, buf=buf, cs=cs: e.tensor_scalar(
                    hq[:, buf, i, :], u_t[:, 2 + cs, :], -128.0, 127.0,
                    OP.max, OP.min), inc=s_hq)
                sl_ut.consumed(2 + cs, s_hq)
            P.wait("tensor", s_hq)
            P.wait("vector", d_rt[buf])
            for hc in range(HC):
                bslot = hc % 4
                sl_bank.acquire("tensor", bslot)
                for i in range(IC):
                    mm(pb[bslot][:, :], wdc[:, i, hc * 128:(hc + 1) * 128],
                       hq[:, buf, i, :], i == 0, i == IC - 1,
                       inc=s_mm if i == IC - 1 else None)
                if hc == 0:
                    run_pending()
                P.wait("vector", s_mm)
                P.op("vector", lambda e, hc=hc, buf=buf, b=bslot:
                     e.scalar_tensor_tensor(
                         rt[:, buf, hc, :], pb[b][:, :], rs_ds[:, hc:hc + 1],
                         rt[:, buf, hc, :], OP.mult, OP.add), inc=s_ev)
                sl_bank.consumed(bslot, s_ev)
                P.wait("sync", s_ev)
                P.dma("sync", dst_hbm[hc * 128:(hc + 1) * 128,
                                      j * 512:(j + 1) * 512],
                      rt[:, buf, hc, :], d_rts[buf])
                if tap is not None:
                    P.dma("sync", tap[hc * 128:(hc + 1) * 128,
                                      j * 512:(j + 1) * 512],
                          rt[:, buf, hc, :], d_rts[buf])
            if fuse_ln2:
                def stats_closure(j=j, buf=buf):
                    P.wait("scalar", s_ev)
                    P.wait("vector", s_ev)
                    ln_stats_tile(rt[:, buf], j, statsh1)
                    bcast_pair(srow[0:1, 1024:1536], srow[0:1, 1536:2048],
                               4, 5, s_row)
                    P.wait("vector", s_bc)
                    for c in range(FC):
                        cs = c % 2
                        sl_ut.acquire("vector", cs)
                        sl_ut.acquire("vector", 2 + cs)
                        P.op("vector", lambda e, c=c, buf=buf, cs=cs:
                             e.tensor_mul(u_t[:, cs, :], rt[:, buf, c, :],
                                          pb[4][:, :]))
                        P.op("vector", lambda e, cs=cs: e.tensor_sub(
                            u_t[:, 2 + cs, :].bitcast(BF16)[:, 0:512],
                            u_t[:, cs, :], pb[5][:, :]), inc=s_vq)
                        sl_ut.consumed(cs, s_vq)
                        sl_ut.consumed(2 + cs, s_vq)
                        P.wait("sync", s_vq)
                        P.dma("sync", xn2h[c, :, j * 512:(j + 1) * 512],
                              u_t[:, 2 + cs, :].bitcast(BF16)[:, 0:512], d_sr)
                        sl_ut.last[2 + cs].append((d_sr, d_sr.n))
                    sl_bank.consumed(4, s_vq)
                    sl_bank.consumed(5, s_vq)
                    sl_rt.consumed(buf, s_vq)
                pending[0] = stats_closure
            sl_rt.consumed(buf, d_rts[buf])
            sl_hload.consumed(buf, s_mm)
        run_pending()

    run_front()
    P.barrier()
    ffn_passB(xT, statsh1, cc_idx=0)
    ffn_passC(xT, r1h, cc_idx=1, store_sem=None, tap=tapd.get("tap_r1"),
              fuse_ln2=True)
    P.barrier()
    if upto in ("ffn1", "ln2"):
        finish(); return _replay()

    for c in range(FC):
        P.dma("gpsimd", xn2[:, c, :], xn2h[c], s_ld)
    sl_w4 = Slots(P, 2)
    for c in range(FC):
        sl_w4.acquire("gpsimd", c % 2)
        P.dma("gpsimd", wst4[:, c % 2, 0:2048], wqT[c * 128:(c + 1) * 128, :],
              d_w4[c % 2])
        P.wait("vector", d_w4[c % 2])
        P.op("vector", lambda e, c=c: e.tensor_copy(
            wqb[:, c, :], wst4[:, c % 2, 0:2048]), inc=s_vq)
        sl_w4.consumed(c % 2, s_vq)
    for c in range(FC):
        sl_w4.acquire("gpsimd", c % 2)
        P.dma("gpsimd", wst4[:, c % 2, 0:LR], wkvT[c * 128:(c + 1) * 128, :],
              d_w4[c % 2])
        P.wait("vector", d_w4[c % 2])
        P.op("vector", lambda e, c=c: e.tensor_copy(
            wkvb[:, c, :], wst4[:, c % 2, 0:LR]), inc=s_vq)
        sl_w4.consumed(c % 2, s_vq)
    sl_w4.acquire("gpsimd", 0)
    P.dma("gpsimd", wst4[0:64, 0, 0:2048], wkuT[:, :], d_w4[0])
    P.wait("vector", d_w4[0])
    P.op("vector", lambda e: e.tensor_copy(wkub[:, :], wst4[0:64, 0, 0:2048]),
         inc=s_vq)
    sl_w4.consumed(0, s_vq)
    sl_w4.acquire("gpsimd", 1)
    P.dma("gpsimd", wst4[0:64, 1, 0:2048], wvuT[:, :], d_w4[1])
    P.wait("vector", d_w4[1])
    P.op("vector", lambda e: e.tensor_copy(wvub[:, :], wst4[0:64, 1, 0:2048]),
         inc=s_vq)
    sl_w4.consumed(1, s_vq)
    P.wait("scalar", s_vq)
    P.op("scalar", lambda e: e.activation(wsq[:, :], wkub[:, :], AF.Square),
         inc=s_sc)
    P.wait("tensor", s_sc)
    for jn in range(4):
        sl_bank.acquire("tensor", 6)
        mm(pb[6][0:1, :], ones64[:, :], wsq[:, jn * 512:(jn + 1) * 512],
           True, True, inc=s_stat)
        P.wait("scalar", s_stat)
        P.op("scalar", lambda e, jn=jn: e.copy(
            srow[0:1, jn * 512:(jn + 1) * 512], pb[6][0:1, :]), inc=s_sc)
        sl_bank.consumed(6, s_sc)
    P.wait("vector", s_sc)
    for h in range(NH):
        P.op("vector", lambda e, h=h: e.reduce_sum(
            sc1[0:1, 16 + h:17 + h], srow[0:1, h * 256:(h + 1) * 256], AX),
            inc=s_vq if h == NH - 1 else None)
    P.barrier()

    P.op("vector", lambda e: e.memset(sc1[0:1, 11:12], 0.0), inc=s_vq)
    for j in range(TT):
        sl_bank.acquire("tensor", 0)
        for c in range(FC):
            mm(pb[0][0:64, :], wkvb[:, c, :], xn2[:, c, j * 512:(j + 1) * 512],
               c == 0, c == FC - 1, inc=s_mm if c == FC - 1 else None)
        P.wait("scalar", s_mm)
        P.op("scalar", lambda e, j=j: e.copy(cb[:, j, :], pb[0][0:64, :]),
             inc=s_sc)
        sl_sq.acquire("scalar", 0)
        P.op("scalar", lambda e: e.activation(
            sq[0:64, 0, :], pb[0][0:64, :], AF.Square), inc=s_sc)
        sl_bank.consumed(0, s_sc)
        sl_bank.acquire("tensor", 1)
        P.wait("tensor", s_sc)
        mm(pb[1][0:1, :], ones64[:, :], sq[0:64, 0, :], True, True, inc=s_stat)
        sl_sq.consumed(0, s_stat)
        P.wait("vector", s_stat)
        P.op("vector", lambda e, j=j: e.reduce_max(
            sc1[0:1, 24 + j:25 + j], pb[1][0:1, :], AX), inc=s_vq)
        sl_bank.consumed(1, s_vq)
    vfence()
    P.op("vector", lambda e: e.reduce_max(
        sc1[0:1, 11:12], sc1[0:1, 24:24 + TT], AX), inc=s_vq)
    vfence()
    P.barrier()
    if upto == "chead":
        finish(); return _replay()

    for h in range(NH):
        P.op("vector", lambda e: e.memset(sc1[0:1, 13:14], 0.0), inc=s_vq)
        sl_qb.acquire("vector", 0)
        pending_qn = [None]

        def run_pending_qn():
            if pending_qn[0] is not None:
                f = pending_qn[0]
                pending_qn[0] = None
                f()

        for j in range(TT):
            qnb = 2 + (j % 2)
            for dh in range(2):
                bslot = dh
                sl_bank.acquire("tensor", bslot)
                for c in range(FC):
                    mm(pb[bslot][:, :],
                       wqb[:, c, h * HD + dh * 128:h * HD + (dh + 1) * 128],
                       xn2[:, c, j * 512:(j + 1) * 512], c == 0, c == FC - 1,
                       inc=s_mm if c == FC - 1 else None)
                run_pending_qn()
                P.wait("vector", s_mm)
                P.op("vector", lambda e, dh=dh, j=j, b=bslot: e.tensor_copy(
                    qb[:, dh, j * 512:(j + 1) * 512], pb[b][:, :]), inc=s_ev)
                sl_bank.consumed(bslot, s_ev)
                if "noqn" in flags:
                    continue
                sqs = (2 * j + dh) % 4
                sl_sq.acquire("scalar", sqs)
                P.wait("scalar", s_ev)
                P.op("scalar", lambda e, j=j, dh=dh, sqs=sqs: e.activation(
                    sq[:, sqs, :], qb[:, dh, j * 512:(j + 1) * 512], AF.Square),
                    inc=s_sc)
                sq_done = s_sc.n

                def qn_closure(j=j, dh=dh, qnb=qnb, sqs=sqs, sq_done=sq_done):
                    if dh == 0:
                        sl_bank.acquire("tensor", qnb)
                    P.wait("tensor", s_sc, sq_done)
                    mm(pb[qnb][0:1, :], ones_b[:, 0:1], sq[:, sqs, :],
                       dh == 0, dh == 1, inc=s_stat if dh == 1 else None)
                    sl_sq.consumed(sqs, s_stat)
                    if dh == 1:
                        P.wait("vector", s_stat)
                        P.op("vector", lambda e, q=qnb, j=j: e.reduce_max(
                            sc1[0:1, 24 + j:25 + j], pb[q][0:1, :], AX),
                            inc=s_vq)
                        sl_bank.consumed(qnb, s_vq)
                pending_qn[0] = qn_closure
        run_pending_qn()
        vfence()
        P.op("vector", lambda e: e.reduce_max(
            sc1[0:1, 13:14], sc1[0:1, 24:24 + TT], AX), inc=s_vq)
        if upto == "qmm":
            break
        vfence()
        P.op("vector", lambda e, h=h: e.tensor_scalar(
            sc1[0:1, 14:15], sc1[0:1, 13:14], sc1[0:1, 16 + h:17 + h],
            sc1[0:1, 11:12], OP.mult, OP.mult), inc=s_vq)
        P.wait("scalar", s_vq)
        P.op("scalar", lambda e: e.activation(sc1[0:1, 15:16], sc1[0:1, 14:15],
                                              AF.Sqrt), inc=s_sc)
        P.wait("vector", s_sc)
        P.op("vector", lambda e: e.tensor_scalar_mul(
            sc1[0:1, 12:13], sc1[0:1, 15:16], -1.0 / 16.0), inc=s_vq)
        vfence()
        sl_bank.acquire("tensor", 4)
        P.wait("tensor", s_vq)
        mm(pb[4][:, 4:5], ones_r[:, :], sc1[0:1, 12:13], True, True, inc=s_bc)
        P.wait("scalar", s_bc)
        P.op("scalar", lambda e: e.copy(svec[:, 4:5], pb[4][:, 4:5]), inc=s_sc)
        sl_bank.consumed(4, s_sc)

        if upto == "qgen":
            break
        sl_kb.acquire("vector", 0)
        for dh in range(2):
            for j in range(TT):
                bslot = (dh * TT + j) % 4
                sl_bank.acquire("tensor", bslot)
                mm(pb[bslot][:, :],
                   wkub[:, h * HD + dh * 128:h * HD + (dh + 1) * 128],
                   cb[:, j, :], True, True, inc=s_mm)
                P.wait("vector", s_mm)
                P.op("vector", lambda e, dh=dh, j=j, b=bslot: e.tensor_copy(
                    kb[:, dh, j * 512:(j + 1) * 512], pb[b][:, :]), inc=s_ev)
                sl_bank.consumed(bslot, s_ev)
        sl_vb.acquire("vector", 0)
        for kc in range(KCN):
            bslot = kc % 4
            j, off = kc // 4, (kc % 4) * 128
            sl_bank.acquire("tensor", bslot)
            mm(pb[bslot][:, 0:256], cb[:, j, off:off + 128],
               wvub[:, h * HD:(h + 1) * HD], True, True, inc=s_mm)
            P.wait("vector", s_mm)
            P.op("vector", lambda e, kc=kc, b=bslot: e.tensor_copy(
                vb[:, kc, :], pb[b][:, 0:256]), inc=s_ev)
            sl_bank.consumed(bslot, s_ev)
        P.barrier(("tensor", "scalar", "vector", "sync"))
        if upto == "kv":
            break

        for jq in range(TT):
            qs = slice(jq * 512, (jq + 1) * 512)
            P.wait("vector", s_stat)
            P.op("vector", lambda e: e.memset(dn[:, :], 0.0), inc=s_vq)
            sl_bank.acquire("tensor", 4)
            sl_bank.acquire("tensor", 5)
            P.wait("tensor", s_vq)

            def emit_scores(kc):
                sslot = kc % 4
                sl_bank.acquire("tensor", sslot)
                for dh in range(2):
                    mm(pb[sslot][:, :], kb[:, dh, kc * 128:(kc + 1) * 128],
                       qb[:, dh, qs], dh == 0, dh == 1,
                       inc=s_mm if dh == 1 else None)
                return s_mm.n

            LOOK = 3
            scores_done = {}
            for k0 in range(min(LOOK, KCN)):
                scores_done[k0] = emit_scores(k0)
            for kc in range(KCN):
                sslot = kc % 4
                pslot = kc % 4
                sl_pt.acquire("scalar", pslot)
                P.wait("scalar", s_mm, scores_done[kc])
                P.op("scalar", lambda e, ss=sslot, ps=pslot: e.activation(
                    pt[:, ps, :], pb[ss][:, :], AF.Exp,
                    bias=svec[:, 4:5], scale=1.0 / 16.0), inc=s_sc)
                exp_done = s_sc.n
                sl_bank.consumed(sslot, s_sc)
                if kc % 2 == 1:
                    P.wait("vector", s_sc, exp_done)
                    P.op("vector", lambda e, ps=pslot: e.tensor_add(
                        dn[:, :], dn[:, :], pt[:, ps - 1, :]))
                    P.op("vector", lambda e, ps=pslot: e.tensor_add(
                        dn[:, :], dn[:, :], pt[:, ps, :]), inc=s_vq)
                    sl_pt.consumed(pslot - 1, s_vq)
                    sl_pt.consumed(pslot, s_vq)
                if kc + LOOK < KCN:
                    scores_done[kc + LOOK] = emit_scores(kc + LOOK)
                P.wait("tensor", s_sc, exp_done)
                for dh in range(2):
                    mm(pb[4 + dh][:, :], vb[:, kc, dh * 128:(dh + 1) * 128],
                       pt[:, pslot, :], kc == 0, kc == KCN - 1,
                       inc=s_mm if dh == 1 else None)
                sl_pt.consumed(pslot, s_mm)
            sl_bank.acquire("tensor", 6)
            P.wait("tensor", s_vq)
            mm(pb[6][0:1, :], ones_b[:, 0:1], dn[:, :], True, True, inc=s_stat)
            sl_srow.acquire("scalar", 0)
            P.wait("scalar", s_stat)
            P.op("scalar", lambda e: e.copy(srow[0:1, 0:512], pb[6][0:1, :]),
                 inc=s_sc)
            sl_bank.consumed(6, s_sc)
            P.wait("vector", s_sc)
            P.op("vector", lambda e: e.reciprocal(srow[0:1, 512:1024],
                                                  srow[0:1, 0:512]), inc=s_vq)
            sl_bank.acquire("tensor", 7)
            P.wait("tensor", s_vq)
            mm(pb[7][:, :], ones_r[:, :], srow[0:1, 512:1024], True, True,
               inc=s_bc)
            sl_srow.consumed(0, s_bc)
            P.wait("vector", s_bc)
            P.op("vector", lambda e: e.tensor_copy(rdnB[:, :], pb[7][:, :]),
                 inc=s_vq)
            sl_bank.consumed(7, s_vq)
            P.wait("vector", s_mm)
            for dh in range(2):
                ob = jq % 2
                sl_obuf.acquire("vector", ob)
                P.op("vector", lambda e, dh=dh, ob=ob: e.tensor_mul(
                    obuf[:, ob, dh, :], pb[4 + dh][:, :], rdnB[:, :]), inc=s_ev)
                sl_bank.consumed(4 + dh, s_ev)
                P.wait("sync", s_ev)
                P.dma("sync", oh[2 * h + dh, jq], obuf[:, ob, dh, :], d_ob[ob])
                sl_obuf.consumed(ob, d_ob[ob])
        sl_qb.consumed(0, s_mm)
        sl_kb.consumed(0, s_mm)
        sl_vb.consumed(0, s_mm)
        if upto in ("head0", "qgen", "kv", "qmm"):
            break

    P.barrier()
    if upto in ("attn", "head0", "qgen", "kv", "qmm"):
        finish(); return _replay()

    sl_wo = Slots(P, 2)
    for cc2 in range(16):
        sl_wo.acquire("gpsimd", cc2 % 2)
        P.dma("gpsimd", wsto[:, cc2 % 2, 0:1024],
              woT[cc2 * 128:(cc2 + 1) * 128, :], d_wo[cc2 % 2])
        P.wait("vector", d_wo[cc2 % 2])
        P.op("vector", lambda e, cc2=cc2: e.tensor_copy(
            wob[:, cc2, :], wsto[:, cc2 % 2, 0:1024]), inc=s_vq)
        sl_wo.consumed(cc2 % 2, s_vq)
    P.op("vector", lambda e: e.memset(smax[:, 4:5], 0.0), inc=s_vq)
    P.barrier()
    pending_wo = [None]

    def run_pending_wo():
        if pending_wo[0] is not None:
            f = pending_wo[0]
            pending_wo[0] = None
            f()

    for j in range(TT):
        buf = j % 2
        sl_ot.acquire("gpsimd", buf)
        for cc2 in range(16):
            P.dma("gpsimd", ot[:, buf, cc2, :], oh[cc2, j], d_ot[buf])
        load_rtile(r1h, j, buf)
        P.wait("tensor", d_ot[buf])
        P.wait("tensor", d_rt[buf])
        for oc in range(HC):
            bslot = oc % 4
            sl_bank.acquire("tensor", bslot)
            for cc2 in range(16):
                mm(pb[bslot][:, :], wob[:, cc2, oc * 128:(oc + 1) * 128],
                   ot[:, buf, cc2, :], cc2 == 0, cc2 == 15,
                   inc=s_mm if cc2 == 15 else None)
            if oc == 0:
                run_pending_wo()
            P.wait("vector", s_mm)
            P.op("vector", lambda e, oc=oc, buf=buf, b=bslot: e.tensor_add(
                rt[:, buf, oc, :], pb[b][:, :], rt[:, buf, oc, :]), inc=s_ev)
            sl_bank.consumed(bslot, s_ev)
            P.wait("sync", s_ev)
            P.dma("sync", r2h[oc * 128:(oc + 1) * 128, j * 512:(j + 1) * 512],
                  rt[:, buf, oc, :], d_rts[buf])
            if "tap_r2" in tapd:
                P.dma("sync", tapd["tap_r2"][oc * 128:(oc + 1) * 128,
                                             j * 512:(j + 1) * 512],
                      rt[:, buf, oc, :], d_rts[buf])
        sl_ot.consumed(buf, s_mm)

        def wo_stats(j=j, buf=buf):
            P.wait("scalar", s_ev)
            P.wait("vector", s_ev)
            ln_stats_tile(rt[:, buf], j, statsh3)
            bcast_pair(srow[0:1, 1024:1536], srow[0:1, 1536:2048], 6, 7, s_row)
            P.wait("vector", s_bc)
            for c in range(FC):
                cs = c % 2
                sl_ut.acquire("vector", cs)
                sl_ut.acquire("vector", 2 + cs)
                P.op("vector", lambda e, c=c, buf=buf, cs=cs: e.tensor_mul(
                    u_t[:, cs, :], rt[:, buf, c, :], pb[6][:, :]))
                P.op("vector", lambda e, cs=cs: e.tensor_sub(
                    u_t[:, 2 + cs, :], u_t[:, cs, :], pb[7][:, :]))
                P.op("vector", lambda e, c=c, cs=cs: e.reduce_max(
                    smax[:, 8 + c:9 + c], u_t[:, 2 + cs, :], AX,
                    apply_absolute_value=True), inc=s_vq)
                sl_ut.consumed(cs, s_vq)
                sl_ut.consumed(2 + cs, s_vq)
            vfence()
            P.op("vector", lambda e, j=j: e.reduce_max(
                smax[:, 16 + j:17 + j], smax[:, 8:16], AX), inc=s_vq)
            sl_rt.consumed(buf, s_vq)
            sl_bank.consumed(6, s_vq)
            sl_bank.consumed(7, s_vq)
        pending_wo[0] = wo_stats
        sl_rt.consumed(buf, d_rts[buf])
    P.wait("gpsimd", s_mm)
    for c in range(FC):
        P.dma("gpsimd", wgc[:, c, :], wgc_h[c], s_ld)
        P.dma("gpsimd", wuc[:, c, :], wuc_h[c], s_ld)
    run_pending_wo()
    vfence()
    P.op("vector", lambda e: e.reduce_max(
        smax[:, 4:5], smax[:, 16:16 + TT], AX), inc=s_vq)
    vfence()
    P.op("vector", lambda e: e.tensor_copy(smax[:, 0:1], smax[:, 4:5]), inc=s_vq)
    ffn_passB(r2h, statsh3, cc_idx=2)
    ffn_passC(r2h, out_h, cc_idx=3, store_sem=None)
    finish()
    return _replay()


def kernel(L0_attn_x, L0_attn_Wq, L0_attn_Wkv_down, L0_attn_Wk_up, L0_attn_Wv_up,
           L0_attn_Wo, L0_ffn_clip_x, L0_ffn_clip_down, L0_ffn_w_gate,
           L0_ffn_w_up, L0_ffn_w_down):
    x = np.asarray(L0_attn_x, dtype=np.float32)
    B, T, _ = x.shape
    c = np.ascontiguousarray
    w = {
        "wqT": c(np.asarray(L0_attn_Wq, np.float32).T),
        "wkvT": c(np.asarray(L0_attn_Wkv_down, np.float32).T),
        "wkuT": c(np.asarray(L0_attn_Wk_up, np.float32).T),
        "wvuT": c(np.asarray(L0_attn_Wv_up, np.float32).T),
        "woT": c(np.asarray(L0_attn_Wo, np.float32).T),
        "wg_rm": c(np.asarray(L0_ffn_w_gate, np.float32)),
        "wu_rm": c(np.asarray(L0_ffn_w_up, np.float32)),
        "wd_rm": c(np.asarray(L0_ffn_w_down, np.float32)),
        "clipx": np.asarray(L0_ffn_clip_x, np.float32).reshape(1, 1),
        "clipd": np.asarray(L0_ffn_clip_down, np.float32).reshape(1, 1),
        "ident": np.eye(128, dtype=np.float32).astype(ml_dtypes.bfloat16),
    }
    nc = build_nc(T=T)
    in_maps = [dict(w, xT=c(x[b].T)) for b in range(B)]
    res = run_bass_kernel_spmd(nc, in_maps, core_ids=list(range(B)))
    out = np.stack([np.asarray(r["out"]).T for r in res.results], axis=0)
    return np.ascontiguousarray(out.astype(np.float32))


# version 14 (final)
# speedup vs baseline: 1.0329x; 1.0329x over previous
import numpy as np
import ml_dtypes
from contextlib import ExitStack

import concourse.bass as bass
import concourse.mybir as mybir
from concourse.bass_utils import run_bass_kernel_spmd

F32 = mybir.dt.float32
BF16 = mybir.dt.bfloat16
AX = mybir.AxisListType.X
OP = mybir.AluOpType
AF = mybir.ActivationFunctionType

D = 1024
NH = 8
HD = 256
LR = 64
I = 2816
FC = D // 128
IC = I // 128
HC = D // 128
QMAX = 127.0
MAGIC = 12582912.0
EPS = 1e-5


class Sem:
    def __init__(self, h, name):
        self.h = h
        self.name = name
        self.n = 0


class Plan:
    ENGINES = ("sync", "scalar", "vector", "tensor", "gpsimd")

    def __init__(self):
        self.ops = {e: [] for e in self.ENGINES}
        self.touched = set()

    def wait(self, eng, s, val=None):
        v = s.n if val is None else val
        assert v <= s.n, f"wait on future value of {s.name}: {v} > {s.n}"
        if v <= 0:
            return
        self.ops[eng].append(lambda e, h=s.h, v=v: e.wait_ge(h, v))

    def op(self, eng, f, inc=None, k=1):
        if inc is not None:
            inc.n += k
            self.touched.add(inc)
            self.ops[eng].append(lambda e, f=f, h=inc.h, k=k: f(e).then_inc(h, k))
        else:
            self.ops[eng].append(f)

    def dma(self, eng, out, in_, s):
        s.n += 16
        self.touched.add(s)
        self.ops[eng].append(
            lambda e, o=out, i=in_, h=s.h: e.dma_start(out=o, in_=i).then_inc(h, 16))

    def barrier(self, engines=None):
        sems = [(s, s.n) for s in self.touched]
        for eng in (engines or self.ENGINES):
            for s, v in sems:
                if v > 0:
                    self.ops[eng].append(lambda e, h=s.h, v=v: e.wait_ge(h, v))


class Slots:

    def __init__(self, plan, n):
        self.P = plan
        self.last = [[] for _ in range(n)]

    def acquire(self, eng, idx=0):
        for s, v in self.last[idx]:
            self.P.wait(eng, s, v)
        self.last[idx] = []

    def consumed(self, idx, sem, val=None):
        self.last[idx].append((sem, sem.n if val is None else val))


def build_nc(T=4096, taps=(), upto='all', flags=()):
    TT = T // 512
    KCN = T // 128
    nc = bass.Bass("TRN2", target_bir_lowering=False,
                   detect_race_conditions=False)

    xT = nc.dram_tensor("xT", [D, T], F32, kind="ExternalInput")
    wqT = nc.dram_tensor("wqT", [D, NH * HD], F32, kind="ExternalInput")
    wkvT = nc.dram_tensor("wkvT", [D, LR], F32, kind="ExternalInput")
    wkuT = nc.dram_tensor("wkuT", [LR, NH * HD], F32, kind="ExternalInput")
    wvuT = nc.dram_tensor("wvuT", [LR, NH * HD], F32, kind="ExternalInput")
    woT = nc.dram_tensor("woT", [NH * HD, D], F32, kind="ExternalInput")
    wg_rm = nc.dram_tensor("wg_rm", [I, D], F32, kind="ExternalInput")
    wu_rm = nc.dram_tensor("wu_rm", [I, D], F32, kind="ExternalInput")
    wd_rm = nc.dram_tensor("wd_rm", [D, I], F32, kind="ExternalInput")
    ident = nc.dram_tensor("ident", [128, 128], BF16, kind="ExternalInput")
    clipx = nc.dram_tensor("clipx", [1, 1], F32, kind="ExternalInput")
    clipd = nc.dram_tensor("clipd", [1, 1], F32, kind="ExternalInput")
    out_h = nc.dram_tensor("out", [D, T], F32, kind="ExternalOutput")

    r1h = nc.dram_tensor("r1h", [D, T], F32)
    r2h = nc.dram_tensor("r2h", [D, T], F32)
    hh = nc.dram_tensor("hh", [IC, TT, 128, 512], BF16)
    oh = nc.dram_tensor("oh", [16, TT, 128, 512], BF16)
    wgc_h = nc.dram_tensor("wgc_h", [FC, 128, I], BF16)
    wuc_h = nc.dram_tensor("wuc_h", [FC, 128, I], BF16)
    wdc_h = nc.dram_tensor("wdc_h", [IC, 128, D], BF16)
    rcph = nc.dram_tensor("rcph", [I], F32)
    smaxh = nc.dram_tensor("smaxh", [128], F32)
    statsh1 = nc.dram_tensor("statsh1", [TT, 1024], F32)
    statsh3 = nc.dram_tensor("statsh3", [TT, 1024], F32)
    xn2h = nc.dram_tensor("xn2h", [FC, 128, T], BF16)
    cc_in = [nc.dram_tensor(f"cc{k}_in", [1, 16], F32) for k in range(4)]
    cc_out = [nc.dram_tensor(f"cc{k}_out", [1, 16], F32) for k in range(4)]
    tapd = {t: nc.dram_tensor(t, [D, T], F32, kind="ExternalOutput")
            for t in taps if t.startswith("tap_")}
    dbg_h = (nc.dram_tensor("dbg", [128, 64], F32, kind="ExternalOutput")
             if "dbg" in taps else None)

    ctx = ExitStack()
    e_ = ctx.enter_context

    ARENA_ELEMS = 76800
    arena = e_(nc.sbuf_tensor("arena", [128, ARENA_ELEMS], BF16))

    def _shp(ap, shape):
        if len(shape) == 1:
            return ap
        pat = ("p (" + " ".join(f"d{i}" for i in range(len(shape))) + ") -> p "
               + " ".join(f"d{i}" for i in range(len(shape))))
        return ap.rearrange(pat, **{f"d{i}": shape[i] for i in range(len(shape))})

    def arb(off, *shape):
        n = int(np.prod(shape))
        return _shp(arena[:, off:off + n], shape)

    def arf(off, *shape):
        n = int(np.prod(shape))
        return _shp(arena[:, off:off + 2 * n].bitcast(F32), shape)

    wgc = arb(0, FC, I)
    wuc = arb(FC * I, FC, I)
    wdc = arb(0, IC, D)
    hload = arb(IC * D, 2, IC, 512)
    hq = hload
    xn2 = arb(0, FC, T)
    wqb = arb(8 * T, FC, NH * HD)
    qb = arb(49152, 2, T)
    kb = arb(57344, 2, T)
    vb = arb(65536, KCN, HD)
    wob = arb(0, 16, D)
    ot = arb(16384, 2, 16, 512)
    wst4 = arf(49152, 2, I)
    wsto = arf(61440, 2, D)
    rt = arf(45056, 2, FC, 512)
    qx = arb(61440, 2, FC, 512)
    gate_s = arf(69632, 4, 512)
    htile = arb(73728, 4, 512)
    wsm = [arf(61440 + k * 2048, 1024) for k in range(4)]
    wbg = [arf(61440, 2816), arf(67072, 2816)]

    sq = e_(nc.sbuf_tensor("sq", [128, 4, 512], BF16))
    rtb = e_(nc.sbuf_tensor("rtb", [128, 4, 512], BF16))
    u_t = e_(nc.sbuf_tensor("u_t", [128, 4, 512], F32))
    srow = e_(nc.sbuf_tensor("srow", [1, 2048], F32))
    cb = e_(nc.sbuf_tensor("cb", [64, TT, 512], BF16))
    wkvb = e_(nc.sbuf_tensor("wkvb", [128, FC, LR], BF16))
    wkub = e_(nc.sbuf_tensor("wkub", [64, NH * HD], BF16))
    wvub = e_(nc.sbuf_tensor("wvub", [64, NH * HD], BF16))
    wsq = e_(nc.sbuf_tensor("wsq", [64, NH * HD], BF16))
    pt = e_(nc.sbuf_tensor("pt", [128, 4, 512], BF16))
    dn = e_(nc.sbuf_tensor("dn", [128, 512], BF16))
    rdnB = e_(nc.sbuf_tensor("rdnB", [128, 512], F32))
    obuf = e_(nc.sbuf_tensor("obuf", [128, 2, 2, 512], BF16))
    rs_g = e_(nc.sbuf_tensor("rs_g", [128, IC], F32))
    rs_u = e_(nc.sbuf_tensor("rs_u", [128, IC], F32))
    rs_d = e_(nc.sbuf_tensor("rs_d", [128, HC], F32))
    rs_gs = e_(nc.sbuf_tensor("rs_gs", [128, IC], F32))
    rs_us = e_(nc.sbuf_tensor("rs_us", [128, IC], F32))
    rs_ds = e_(nc.sbuf_tensor("rs_ds", [128, HC], F32))
    smax = e_(nc.sbuf_tensor("smax", [128, 24], F32))
    svec = e_(nc.sbuf_tensor("svec", [128, 8], F32))
    sc1 = e_(nc.sbuf_tensor("sc1", [1, 32], F32))
    ccst = e_(nc.sbuf_tensor("ccst", [1, 16], F32))
    ones_b = e_(nc.sbuf_tensor("ones_b", [128, 2], BF16))
    cvec = e_(nc.sbuf_tensor("cvec", [128, 3], F32))
    nc.const_aps.register(F32, float(MAGIC), cvec[:, 1:2])
    nc.const_aps.register(F32, float(-MAGIC), cvec[:, 2:3])
    ones_r = e_(nc.sbuf_tensor("ones_r", [1, 128], F32))
    idb = e_(nc.sbuf_tensor("idb", [128, 128], BF16))
    p0s = e_(nc.sbuf_tensor("p0s", [128, 4], F32))
    ones64 = e_(nc.sbuf_tensor("ones64", [64, 1], BF16))

    pb = [e_(nc.psum_tensor(f"pb{i}", [128, 512], F32)) for i in range(8)]

    P = Plan()

    def S_(name):
        return Sem(e_(nc.semaphore(name)), name)

    s_ld = S_("ld")
    s_hq = S_("hq")
    s_ld2 = S_("ld2")
    s_st = S_("st")

    s_mm = S_("mm")
    s_ev = S_("ev")
    s_vq = S_("vq")
    s_sc = S_("sc")
    s_row = S_("row")
    s_cc = S_("cc")
    s_stat = S_("stat")
    s_bc = S_("bc")
    s_out = S_("out")
    d_ws = [S_(f"dws{k}") for k in range(4)]
    d_wb = [S_("dwb0"), S_("dwb1")]
    d_rt = [S_("drt0"), S_("drt1")]
    d_rts = [S_("drts0"), S_("drts1")]
    d_hl = [S_("dhl0"), S_("dhl1")]
    d_ht = [S_(f"dht{i}") for i in range(4)]
    d_ob = [S_("dob0"), S_("dob1")]
    d_ot = [S_("dot0"), S_("dot1")]
    d_w4 = [S_("dw40"), S_("dw41")]
    d_wo = [S_("dwo0"), S_("dwo1")]
    d_cc = S_("dcc")
    d_ccs = S_("dccs")
    d_sr = S_("dsr")
    d_clip = S_("dclip")

    blk = e_(nc.Block())

    def finish():
        if dbg_h is not None:
            P.wait("sync", s_vq)
            P.dma("sync", dbg_h[0:1, 0:32], sc1[:, :], s_st)
            P.dma("sync", dbg_h[:, 32:40], smax[:, 0:8], s_st)
            P.dma("sync", dbg_h[:, 40:44], svec[:, 0:4], s_st)
            P.dma("sync", dbg_h[:, 48:56], rs_gs[:, 0:8], s_st)
            P.dma("sync", dbg_h[:, 56:64], rs_ds[:, 0:8], s_st)
        P.wait("sync", d_rts[0])
        P.wait("sync", d_rts[1])
        P.wait("sync", s_st)

    def _replay():
        @blk.sync
        def _(eng):
            for f in P.ops["sync"]:
                f(eng)

        @blk.scalar
        def _(eng):
            for f in P.ops["scalar"]:
                f(eng)

        @blk.vector
        def _(eng):
            for f in P.ops["vector"]:
                f(eng)

        @blk.tensor
        def _(eng):
            for f in P.ops["tensor"]:
                f(eng)

        @blk.gpsimd
        def _(eng):
            for f in P.ops["gpsimd"]:
                f(eng)

        ctx.close()
        return nc

    def vfence():
        P.wait("vector", s_vq)

    def mm(psum_ap, lhsT, rhs, start, stop, inc=None, k=1):
        P.op("tensor",
             lambda e, o=psum_ap, l=lhsT, r=rhs, a=start, b=stop: e.matmul(
                 o, l, r, start=a, stop=b), inc=inc, k=k)

    sl_rt = Slots(P, 2)
    sl_qx = Slots(P, 2)
    sl_hload = Slots(P, 2)
    sl_gate = Slots(P, 4)
    sl_htile = Slots(P, 4)
    sl_ut = Slots(P, 4)
    sl_sq = Slots(P, 4)
    sl_rtb = Slots(P, 4)
    sl_wst = Slots(P, 2)
    sl_pt = Slots(P, 4)
    sl_ot = Slots(P, 2)
    sl_obuf = Slots(P, 2)
    sl_bank = Slots(P, 8)
    sl_stage = Slots(P, 1)
    sl_srow = Slots(P, 1)
    sl_qb = Slots(P, 1)
    sl_kb = Slots(P, 1)
    sl_vb = Slots(P, 1)

    P.op("vector", lambda e: e.memset(ones_b[:, 0:1], 1.0))
    P.op("vector", lambda e: e.memset(ones_b[:, 1:2], 1.0 / D))
    P.op("vector", lambda e: e.memset(ones_r[:, :], 1.0))
    P.op("vector", lambda e: e.memset(ones64[:, :], 1.0))
    P.op("vector", lambda e: e.memset(smax[:, :], 0.0))
    P.op("vector", lambda e: e.memset(cvec[:, 0:1], EPS))
    P.op("vector", lambda e: e.memset(cvec[:, 1:2], MAGIC))
    P.op("vector", lambda e: e.memset(cvec[:, 2:3], -MAGIC))
    P.op("vector", lambda e: e.memset(sc1[:, :], 0.0), inc=s_vq)

    P.wait("sync", s_vq)
    P.wait("tensor", s_vq)
    P.dma("sync", sc1[0:1, 2:3], clipx[:, :], d_clip)
    P.dma("sync", sc1[0:1, 3:4], clipd[:, :], d_clip)
    P.wait("scalar", d_clip)
    P.wait("scalar", s_vq)
    P.op("scalar", lambda e: e.activation(sc1[0:1, 0:1], sc1[0:1, 2:3], AF.Sigmoid))
    P.op("scalar", lambda e: e.activation(sc1[0:1, 1:2], sc1[0:1, 3:4], AF.Sigmoid),
         inc=s_sc)
    sig_done = (s_sc, s_sc.n)

    sl_wsm = Slots(P, 4)
    sl_wbg = Slots(P, 2)
    sl_p0s = Slots(P, 2)
    WQ = ("gpsimd", "sync")
    cnt = {"pt": 0, "bk": 0, "ob": 0, "bgfirst": True}
    pend_evac = [None]

    def run_evac():
        if pend_evac[0] is not None:
            f = pend_evac[0]
            pend_evac[0] = None
            f()

    def em_ident():
        P.dma("sync", idb[:, :], ident[:, :], s_ld2)
        P.wait("tensor", s_ld2)

    def em_tchunk(w_rm, rs_t, c, ncol, dst3, big):
        nj = (ncol + 511) // 512
        if big:
            k = c % 2
            wap, dsem, sl = wbg[k], d_wb[k], sl_wbg
        else:
            k = c % 4
            wap, dsem, sl = wsm[k], d_ws[k], sl_wsm
        q = WQ[k % 2]
        sl.acquire(q, k)
        if big and cnt["bgfirst"]:
            P.wait(q, s_sc)
            if k == 1:
                cnt["bgfirst"] = False
        P.dma(q, wap[:, 0:ncol], w_rm[c * 128:(c + 1) * 128, :], dsem)
        P.wait("vector", dsem)
        P.op("vector", lambda e, c=c, r=rs_t, n=ncol, w=wap: e.reduce_max(
            r[:, c:c + 1], w[:, 0:n], AX,
            apply_absolute_value=True), inc=s_vq)
        vfence()
        pp = c % 2
        sl_p0s.acquire("vector", pp)
        P.op("vector", lambda e, c=c, r=rs_t, pp=pp: e.tensor_scalar(
            p0s[:, 2 * pp:2 * pp + 1], r[:, c:c + 1], 1.0 / QMAX, None,
            OP.mult), inc=s_vq)
        vfence()
        P.op("vector", lambda e, pp=pp: e.reciprocal(
            p0s[:, 2 * pp + 1:2 * pp + 2], p0s[:, 2 * pp:2 * pp + 1]),
            inc=s_vq)
        P.wait("scalar", s_vq)
        for jn in range(nj):
            w = min(512, ncol - jn * 512)
            us = jn % 2
            sl_ut.acquire("scalar", us)
            P.op("scalar", lambda e, us=us, w=w, jn=jn, wap=wap, pp=pp:
                 e.activation(u_t[:, us, 0:w],
                              wap[:, jn * 512:jn * 512 + w], AF.Copy,
                              scale=p0s[:, 2 * pp + 1:2 * pp + 2],
                              bias=float(MAGIC)))
            ps = cnt["pt"] % 4
            cnt["pt"] += 1
            sl_pt.acquire("scalar", ps)
            P.op("scalar", lambda e, us=us, w=w, ps=ps: e.activation(
                pt[:, ps, 0:w], u_t[:, us, 0:w], AF.Copy,
                bias=float(-MAGIC)), inc=s_sc)
            sl_ut.consumed(us, s_sc)
            if jn == nj - 1:
                sl.consumed(k, s_sc)
                sl_p0s.consumed(pp, s_sc)
            nb = (w + 127) // 128
            b = cnt["bk"] % 4
            cnt["bk"] += 1
            sl_bank.acquire("tensor", b)
            P.wait("tensor", s_sc)
            for kk in range(nb):
                mm(pb[b][:, kk * 128:(kk + 1) * 128],
                   pt[:, ps, kk * 128:(kk + 1) * 128], idb[:, :],
                   True, True, inc=s_mm if kk == nb - 1 else None)
            sl_pt.consumed(ps, s_mm)
            mmv = s_mm.n
            run_evac()

            def evac(jn=jn, nb=nb, c=c, b=b, mmv=mmv, dst3=dst3):
                P.wait("vector", s_mm, mmv)
                if dst3 is not None:
                    P.op("vector", lambda e, d=dst3, jn=jn, nb=nb, c=c, b=b:
                         e.tensor_copy(
                             d[:, jn * 4:jn * 4 + nb, c * 128:(c + 1) * 128],
                             pb[b][:, 0:nb * 128]), inc=s_ev)
                    sl_bank.consumed(b, s_ev)
                else:
                    ob = cnt["ob"] % 2
                    cnt["ob"] += 1
                    sl_obuf.acquire("vector", ob)
                    P.op("vector", lambda e, ob=ob, nb=nb, b=b: e.tensor_copy(
                        obuf[:, ob, 0, 0:nb * 128], pb[b][:, 0:nb * 128]),
                        inc=s_ev)
                    sl_bank.consumed(b, s_ev)
                    P.wait("sync", s_ev)
                    for kk in range(nb):
                        P.dma("sync",
                              wdc_h[jn * 4 + kk][:, c * 128:(c + 1) * 128],
                              obuf[:, ob, 0, kk * 128:(kk + 1) * 128],
                              d_ob[ob])
                    sl_obuf.consumed(ob, d_ob[ob])
            pend_evac[0] = evac

    def em_spill_codes(dst3, spill_h):
        run_evac()
        P.wait("sync", s_ev)
        for cd in range(FC):
            P.dma("sync", spill_h[cd], dst3[:, cd, :], s_st)

    p0_list = [em_ident]
    for c in range(IC):
        p0_list.append(lambda c=c: em_tchunk(wg_rm, rs_g, c, D, wgc, False))
    p0_list.append(lambda: em_spill_codes(wgc, wgc_h))
    for c in range(IC):
        p0_list.append(lambda c=c: em_tchunk(wu_rm, rs_u, c, D, wuc, False))
    p0_list.append(lambda: em_spill_codes(wuc, wuc_h))
    for c in range(HC):
        p0_list.append(lambda c=c: em_tchunk(wd_rm, rs_d, c, I, None, True))
    p0_list.append(run_evac)

    def ln_stats_p1(src_tile, j):
        sl_bank.acquire("tensor", 6)
        sl_bank.acquire("tensor", 7)
        for g in range(FC // 4):
            for c4 in range(4):
                c = g * 4 + c4
                sl_sq.acquire("scalar", c4)
                P.op("scalar", lambda e, c=c, c4=c4, s=src_tile: e.activation(
                    sq[:, c4, :], s[:, c, :], AF.Square), inc=s_sc)
                sl_rtb.acquire("scalar", c4)
                P.op("scalar", lambda e, c=c, c4=c4, s=src_tile: e.copy(
                    rtb[:, c4, :], s[:, c, :]), inc=s_sc)
            P.wait("tensor", s_sc)
            for c4 in range(4):
                c = g * 4 + c4
                mm(pb[6][0:1, :], ones_b[:, 1:2], rtb[:, c4, :],
                   c == 0, c == FC - 1)
                mm(pb[7][0:1, :], ones_b[:, 1:2], sq[:, c4, :],
                   c == 0, c == FC - 1, inc=s_stat)
            for c4 in range(4):
                sl_sq.consumed(c4, s_stat)
                sl_rtb.consumed(c4, s_stat)

    def ln_stats_p2(j, stats_dst):
        sl_srow.acquire("scalar", 0)
        P.wait("scalar", s_stat)
        P.op("scalar", lambda e: e.activation(srow[0:1, 0:512], pb[6][0:1, :],
                                              AF.Square), inc=s_sc)
        P.wait("vector", s_sc)
        P.op("vector", lambda e: e.tensor_sub(srow[0:1, 512:1024], pb[7][0:1, :],
                                              srow[0:1, 0:512]), inc=s_vq)
        P.wait("scalar", s_vq)
        P.op("scalar", lambda e: e.activation(srow[0:1, 0:512],
                                              srow[0:1, 512:1024], AF.Sqrt,
                                              bias=cvec[0:1, 0:1]), inc=s_sc)
        P.wait("vector", s_sc)
        P.op("vector", lambda e: e.reciprocal(
            srow[0:1, 1024:1536], srow[0:1, 0:512]))
        P.op("vector", lambda e: e.tensor_mul(
            srow[0:1, 1536:2048], pb[6][0:1, :], srow[0:1, 1024:1536]),
            inc=s_row)
        sl_bank.consumed(6, s_row)
        sl_bank.consumed(7, s_row)
        P.wait("sync", s_row)
        P.dma("sync", stats_dst[j], srow[0:1, 1024:2048], d_sr)
        sl_srow.consumed(0, d_sr)

    def ln_stats_tile(src_tile, j, stats_dst):
        ln_stats_p1(src_tile, j)
        ln_stats_p2(j, stats_dst)

    def restore_stats(stats_src, j, scale_slot):
        sl_srow.acquire("sync", 0)
        P.dma("sync", srow[0:1, 1024:2048], stats_src[j], s_ld2)
        P.wait("vector", s_ld2)
        P.op("vector", lambda e, ss=scale_slot: e.tensor_scalar(
            srow[0:1, 0:512], srow[0:1, 1024:1536], sc1[0:1, ss:ss + 1],
            None, OP.mult))
        P.op("vector", lambda e, ss=scale_slot: e.tensor_scalar(
            srow[0:1, 512:1024], srow[0:1, 1536:2048], sc1[0:1, ss:ss + 1],
            None, OP.mult), inc=s_row)

    def bcast_pair(row_a, row_b, ia, ib, gate_sem):
        sl_bank.acquire("tensor", ia)
        sl_bank.acquire("tensor", ib)
        P.wait("tensor", gate_sem)
        mm(pb[ia][:, :], ones_r[:, :], row_a, True, True)
        mm(pb[ib][:, :], ones_r[:, :], row_b, True, True, inc=s_bc)
        sl_srow.consumed(0, s_bc)

    def cross_part_max(src_col, dst_slot):
        P.wait("sync", s_vq)
        P.wait("sync", s_ev)
        sl_srow.acquire("sync", 0)
        P.dma("sync", smaxh[0:128], src_col, s_st)
        P.wait("sync", s_st)
        P.dma("sync", srow[0:1, 0:128], smaxh[0:128], s_ld2)
        sl_srow.consumed(0, s_vq, s_vq.n + 1)
        P.wait("vector", s_ld2)
        P.op("vector", lambda e, d=dst_slot: e.reduce_max(
            sc1[0:1, d:d + 1], srow[0:1, 0:128], AX), inc=s_vq)
        vfence()

    def collective_max(idx, src_slot, dst_slot):
        for k in range(16):
            P.op("vector", lambda e, k=k, s=src_slot: e.tensor_copy(
                ccst[0:1, k:k + 1], sc1[0:1, s:s + 1]),
                inc=s_vq if k == 15 else None)
        P.wait("sync", s_vq)
        P.dma("sync", cc_in[idx][:, :], ccst[:, :], d_ccs)
        P.wait("gpsimd", d_ccs)
        P.op("gpsimd", lambda e, idx=idx: e.collective_compute(
            "AllReduce", OP.max, replica_groups=[list(range(8))],
            ins=[cc_in[idx].ap().opt()], outs=[cc_out[idx].ap().opt()]),
            inc=s_cc)
        P.wait("gpsimd", s_cc)
        P.dma("gpsimd", ccst[:, :], cc_out[idx][:, :], d_cc)
        P.wait("vector", d_cc)
        P.op("vector", lambda e, d=dst_slot: e.tensor_copy(
            sc1[0:1, d:d + 1], ccst[0:1, 8:9]), inc=s_vq)
        vfence()

    def make_scale(gmax_slot, sig_slot, scale_slot, inv_slot, svec_off):
        P.wait("vector", sig_done[0], sig_done[1])
        P.op("vector", lambda e, g=gmax_slot, s=sig_slot, o=scale_slot:
             e.tensor_scalar(sc1[0:1, o:o + 1], sc1[0:1, g:g + 1],
                             sc1[0:1, s:s + 1], 1.0 / QMAX, OP.mult, OP.mult),
             inc=s_vq)
        vfence()
        P.op("vector", lambda e, o=scale_slot, v=inv_slot: e.reciprocal(
            sc1[0:1, v:v + 1], sc1[0:1, o:o + 1]), inc=s_vq)
        vfence()
        sl_bank.acquire("tensor", 4)
        P.wait("tensor", s_vq)
        mm(pb[4][:, svec_off:svec_off + 1], ones_r[:, :],
           sc1[0:1, scale_slot:scale_slot + 1], True, True)
        mm(pb[4][:, svec_off + 1:svec_off + 2], ones_r[:, :],
           sc1[0:1, inv_slot:inv_slot + 1], True, True, inc=s_bc)
        P.wait("scalar", s_bc)
        P.op("scalar", lambda e, o=svec_off: e.copy(
            svec[:, o:o + 2], pb[4][:, o:o + 2]), inc=s_sc)
        sl_bank.consumed(4, s_sc)

    def load_rtile(src_hbm, j, buf):
        sl_rt.acquire("gpsimd", buf)
        for c in range(FC):
            P.dma("gpsimd", rt[:, buf, c, :],
                  src_hbm[c * 128:(c + 1) * 128, j * 512:(j + 1) * 512],
                  d_rt[buf])

    def em_pa_A(src_hbm, j):
        buf = j % 2
        load_rtile(src_hbm, j, buf)
        P.wait("scalar", d_rt[buf])
        ln_stats_p1(rt[:, buf], j)

    def em_pa_B(stats_dst, j):
        ln_stats_p2(j, stats_dst)

    def em_pa_C(j):
        buf = j % 2
        bcast_pair(srow[0:1, 1024:1536], srow[0:1, 1536:2048], 4, 5, s_row)
        P.wait("vector", s_bc)
        P.wait("vector", d_rt[buf])
        for c in range(FC):
            cs = 2 + (c % 2)
            sl_ut.acquire("vector", cs)
            P.op("vector", lambda e, c=c, buf=buf, cs=cs: e.tensor_mul(
                u_t[:, cs, :], rt[:, buf, c, :], pb[4][:, :]))
            P.op("vector", lambda e, cs=cs: e.tensor_sub(
                u_t[:, cs, :], u_t[:, cs, :], pb[5][:, :]))
            P.op("vector", lambda e, c=c, cs=cs: e.reduce_max(
                smax[:, 8 + c:9 + c], u_t[:, cs, :], AX,
                apply_absolute_value=True), inc=s_vq)
            sl_ut.consumed(cs, s_vq)
        vfence()
        P.op("vector", lambda e, j=j: e.reduce_max(
            smax[:, 16 + j:17 + j], smax[:, 8:16], AX), inc=s_vq)
        sl_rt.consumed(buf, s_vq)
        sl_bank.consumed(4, s_vq)
        sl_bank.consumed(5, s_vq)

    def run_front():
        pa_list = [lambda: P.op("vector",
                                lambda e: e.memset(smax[:, 0:1], 0.0),
                                inc=s_vq)]
        for j in range(TT):
            pa_list.append(lambda j=j: em_pa_A(xT, j))
            pa_list.append(lambda j=j: em_pa_B(statsh1, j))
            pa_list.append(lambda j=j: em_pa_C(j))

        def pa_final():
            vfence()
            P.op("vector", lambda e: e.reduce_max(
                smax[:, 0:1], smax[:, 16:16 + TT], AX), inc=s_vq)
        pa_list.append(pa_final)
        n0, n1 = len(p0_list), len(pa_list)
        k = 0
        for i, f in enumerate(p0_list):
            f()
            if k < n1 and (i + 1) * (n1 + 1) >= (k + 1) * n0:
                pa_list[k]()
                k += 1
        while k < n1:
            pa_list[k]()
            k += 1

    def ffn_passB(src_hbm, stats_src, cc_idx):
        P.wait("tensor", s_ld)
        cross_part_max(smax[:, 0:1], 4)
        collective_max(cc_idx, 4, 5)
        make_scale(5, 0, 6, 7, 0)
        P.wait("vector", s_sc)
        P.op("vector", lambda e: e.tensor_scalar(rs_gs[:, 0:IC], rs_g[:, 0:IC],
                                                 svec[:, 0:1], 1.0 / QMAX,
                                                 OP.mult, OP.mult))
        P.op("vector", lambda e: e.tensor_scalar(rs_us[:, 0:IC], rs_u[:, 0:IC],
                                                 svec[:, 0:1], 1.0 / QMAX,
                                                 OP.mult, OP.mult), inc=s_vq)
        P.op("vector", lambda e: e.memset(smax[:, 2:3], 0.0), inc=s_vq)
        P.barrier()
        for j in range(TT):
            buf = j % 2
            load_rtile(src_hbm, j, buf)
            restore_stats(stats_src, j, scale_slot=7)
            bcast_pair(srow[0:1, 0:512], srow[0:1, 512:1024], 4, 5, s_row)
            P.wait("vector", s_bc)
            P.wait("vector", d_rt[buf])
            sl_qx.acquire("vector", buf)
            for c in range(FC):
                cs = c % 2
                sl_ut.acquire("vector", cs)
                sl_ut.acquire("vector", 2 + cs)
                P.op("vector", lambda e, c=c, buf=buf, cs=cs: e.tensor_mul(
                    u_t[:, cs, :], rt[:, buf, c, :], pb[4][:, :]))
                P.op("vector", lambda e, cs=cs: e.tensor_sub(
                    u_t[:, 2 + cs, :], u_t[:, cs, :], pb[5][:, :]))
                P.op("vector", lambda e, cs=cs: e.tensor_scalar(
                    u_t[:, cs, :], u_t[:, 2 + cs, :], MAGIC, MAGIC - 128.0,
                    OP.add, OP.max))
                P.op("vector", lambda e, c=c, buf=buf, cs=cs: e.tensor_scalar(
                    qx[:, buf, c, :], u_t[:, cs, :], MAGIC + 127.0, MAGIC,
                    OP.min, OP.subtract), inc=s_vq)
                sl_ut.consumed(cs, s_vq)
                sl_ut.consumed(2 + cs, s_vq)
            sl_rt.consumed(buf, s_vq)
            sl_bank.consumed(4, s_vq)
            sl_bank.consumed(5, s_vq)
            P.wait("tensor", s_vq)
            for i in range(IC):
                for half, wc in enumerate((wgc, wuc)):
                    bslot = (2 * i + half) % 4
                    sl_bank.acquire("tensor", bslot)
                    for c in range(FC):
                        mm(pb[bslot][:, :], wc[:, c, i * 128:(i + 1) * 128],
                           qx[:, buf, c, :], c == 0, c == FC - 1,
                           inc=s_mm if c == FC - 1 else None)
                gate_done = s_mm.n - 1
                up_done = s_mm.n
                gslot = (2 * i) % 4
                uslot = (2 * i + 1) % 4
                hslot = i % 4
                sl_gate.acquire("scalar", hslot)
                P.wait("scalar", s_mm, gate_done)
                P.op("scalar", lambda e, i=i, g=gslot, hs=hslot: e.activation(
                    gate_s[:, hs, :], pb[g][:, :], AF.Silu,
                    scale=rs_gs[:, i:i + 1]), inc=s_sc)
                sl_bank.consumed(gslot, s_sc)
                sl_htile.acquire("vector", hslot)
                P.wait("vector", s_mm, up_done)
                P.wait("vector", s_sc)
                P.op("vector", lambda e, i=i, u=uslot, hs=hslot:
                     e.scalar_tensor_tensor(
                         htile[:, hs, :], pb[u][:, :], rs_us[:, i:i + 1],
                         gate_s[:, hs, :], OP.mult, OP.mult))
                P.op("vector", lambda e, hs=hslot, i=i: e.reduce_max(
                    smax[:, 8 + i % 8:9 + i % 8], htile[:, hs, :], AX,
                    apply_absolute_value=True), inc=s_ev)
                sl_bank.consumed(uslot, s_ev)
                sl_gate.consumed(hslot, s_ev)
                if i % 8 == 7 or i == IC - 1:
                    g = (i % 8) + 1
                    vfence()
                    P.wait("vector", s_ev)
                    P.op("vector", lambda e, g=g: e.reduce_max(
                        smax[:, 3:4], smax[:, 8:8 + g], AX), inc=s_vq)
                    vfence()
                    P.op("vector", lambda e: e.tensor_max(
                        smax[:, 2:3], smax[:, 2:3], smax[:, 3:4]), inc=s_vq)
                    vfence()
                P.wait("sync", s_ev)
                P.dma("sync", hh[i, j], htile[:, hslot, :], d_ht[hslot])
                sl_htile.consumed(hslot, d_ht[hslot])
            sl_qx.consumed(buf, s_mm)

    def ffn_passC(src_hbm, dst_hbm, cc_idx, store_sem, tap=None,
                  fuse_ln2=False):
        P.wait("gpsimd", s_mm)
        P.wait("gpsimd", s_st)
        P.wait("gpsimd", d_ob[0])
        P.wait("gpsimd", d_ob[1])
        for c in range(IC):
            P.dma("gpsimd", wdc[:, c, :], wdc_h[c], s_ld)
        cross_part_max(smax[:, 2:3], 8)
        collective_max(cc_idx, 8, 9)
        make_scale(9, 1, 10, 11, 2)
        P.wait("vector", s_sc)
        P.op("vector", lambda e: e.tensor_scalar(
            rs_ds[:, 0:HC], rs_d[:, 0:HC], svec[:, 2:3], 0.5 / QMAX,
            OP.mult, OP.mult), inc=s_vq)
        P.barrier()
        pending = [None]

        def run_pending():
            if pending[0] is not None:
                f = pending[0]
                pending[0] = None
                f()

        for j in range(TT):
            buf = j % 2
            sl_hload.acquire("gpsimd", buf)
            for i in range(IC):
                P.dma("gpsimd", hload[:, buf, i, :], hh[i, j], d_hl[buf])
            load_rtile(src_hbm, j, buf)
            P.wait("scalar", d_hl[buf])
            for i in range(IC):
                cs = i % 2
                sl_ut.acquire("scalar", cs)
                P.op("scalar", lambda e, i=i, buf=buf, cs=cs: e.activation(
                    u_t[:, cs, :], hload[:, buf, i, :], AF.Copy,
                    bias=float(MAGIC), scale=svec[:, 3:4]), inc=s_sc)
                sl_ut.acquire("vector", 2 + cs)
                P.wait("vector", s_sc)
                P.op("vector", lambda e, cs=cs: e.tensor_scalar(
                    u_t[:, 2 + cs, :], u_t[:, cs, :], MAGIC - 128.0,
                    MAGIC + 127.0, OP.max, OP.min))
                sl_ut.consumed(cs, s_hq, s_hq.n + 1)
                P.op("vector", lambda e, i=i, buf=buf, cs=cs: e.tensor_scalar(
                    hq[:, buf, i, :], u_t[:, 2 + cs, :], MAGIC, None,
                    OP.subtract), inc=s_hq)
                sl_ut.consumed(2 + cs, s_hq)
            P.wait("tensor", s_hq)
            P.wait("vector", d_rt[buf])
            for hc in range(HC):
                bslot = hc % 4
                sl_bank.acquire("tensor", bslot)
                for i in range(IC):
                    mm(pb[bslot][:, :], wdc[:, i, hc * 128:(hc + 1) * 128],
                       hq[:, buf, i, :], i == 0, i == IC - 1,
                       inc=s_mm if i == IC - 1 else None)
                if hc == 0:
                    run_pending()
                P.wait("vector", s_mm)
                P.op("vector", lambda e, hc=hc, buf=buf, b=bslot:
                     e.scalar_tensor_tensor(
                         rt[:, buf, hc, :], pb[b][:, :], rs_ds[:, hc:hc + 1],
                         rt[:, buf, hc, :], OP.mult, OP.add), inc=s_ev)
                sl_bank.consumed(bslot, s_ev)
                P.wait("sync", s_ev)
                P.dma("sync", dst_hbm[hc * 128:(hc + 1) * 128,
                                      j * 512:(j + 1) * 512],
                      rt[:, buf, hc, :], d_rts[buf])
                if tap is not None:
                    P.dma("sync", tap[hc * 128:(hc + 1) * 128,
                                      j * 512:(j + 1) * 512],
                          rt[:, buf, hc, :], d_rts[buf])
            if fuse_ln2:
                def stats_closure(j=j, buf=buf):
                    P.wait("scalar", s_ev)
                    P.wait("vector", s_ev)
                    ln_stats_tile(rt[:, buf], j, statsh1)
                    bcast_pair(srow[0:1, 1024:1536], srow[0:1, 1536:2048],
                               4, 5, s_row)
                    P.wait("vector", s_bc)
                    for c in range(FC):
                        cs = c % 2
                        sl_ut.acquire("vector", cs)
                        sl_ut.acquire("vector", 2 + cs)
                        P.op("vector", lambda e, c=c, buf=buf, cs=cs:
                             e.tensor_mul(u_t[:, cs, :], rt[:, buf, c, :],
                                          pb[4][:, :]))
                        P.op("vector", lambda e, cs=cs: e.tensor_sub(
                            u_t[:, 2 + cs, :].bitcast(BF16)[:, 0:512],
                            u_t[:, cs, :], pb[5][:, :]), inc=s_vq)
                        sl_ut.consumed(cs, s_vq)
                        sl_ut.consumed(2 + cs, s_vq)
                        P.wait("sync", s_vq)
                        P.dma("sync", xn2h[c, :, j * 512:(j + 1) * 512],
                              u_t[:, 2 + cs, :].bitcast(BF16)[:, 0:512], d_sr)
                        sl_ut.last[2 + cs].append((d_sr, d_sr.n))
                    sl_bank.consumed(4, s_vq)
                    sl_bank.consumed(5, s_vq)
                    sl_rt.consumed(buf, s_vq)
                pending[0] = stats_closure
            sl_rt.consumed(buf, d_rts[buf])
            sl_hload.consumed(buf, s_mm)
        run_pending()

    run_front()
    P.barrier()
    ffn_passB(xT, statsh1, cc_idx=0)
    ffn_passC(xT, r1h, cc_idx=1, store_sem=None, tap=tapd.get("tap_r1"),
              fuse_ln2=True)
    P.barrier()
    if upto in ("ffn1", "ln2"):
        finish(); return _replay()

    for c in range(FC):
        P.dma("gpsimd", xn2[:, c, :], xn2h[c], s_ld)
    sl_w4 = Slots(P, 2)
    for c in range(FC):
        sl_w4.acquire("gpsimd", c % 2)
        P.dma("gpsimd", wst4[:, c % 2, 0:2048], wqT[c * 128:(c + 1) * 128, :],
              d_w4[c % 2])
        P.wait("vector", d_w4[c % 2])
        P.op("vector", lambda e, c=c: e.tensor_copy(
            wqb[:, c, :], wst4[:, c % 2, 0:2048]), inc=s_vq)
        sl_w4.consumed(c % 2, s_vq)
    for c in range(FC):
        sl_w4.acquire("gpsimd", c % 2)
        P.dma("gpsimd", wst4[:, c % 2, 0:LR], wkvT[c * 128:(c + 1) * 128, :],
              d_w4[c % 2])
        P.wait("vector", d_w4[c % 2])
        P.op("vector", lambda e, c=c: e.tensor_copy(
            wkvb[:, c, :], wst4[:, c % 2, 0:LR]), inc=s_vq)
        sl_w4.consumed(c % 2, s_vq)
    sl_w4.acquire("gpsimd", 0)
    P.dma("gpsimd", wst4[0:64, 0, 0:2048], wkuT[:, :], d_w4[0])
    P.wait("vector", d_w4[0])
    P.op("vector", lambda e: e.tensor_copy(wkub[:, :], wst4[0:64, 0, 0:2048]),
         inc=s_vq)
    sl_w4.consumed(0, s_vq)
    sl_w4.acquire("gpsimd", 1)
    P.dma("gpsimd", wst4[0:64, 1, 0:2048], wvuT[:, :], d_w4[1])
    P.wait("vector", d_w4[1])
    P.op("vector", lambda e: e.tensor_copy(wvub[:, :], wst4[0:64, 1, 0:2048]),
         inc=s_vq)
    sl_w4.consumed(1, s_vq)
    P.wait("scalar", s_vq)
    P.op("scalar", lambda e: e.activation(wsq[:, :], wkub[:, :], AF.Square),
         inc=s_sc)
    P.wait("tensor", s_sc)
    for jn in range(4):
        sl_bank.acquire("tensor", 6)
        mm(pb[6][0:1, :], ones64[:, :], wsq[:, jn * 512:(jn + 1) * 512],
           True, True, inc=s_stat)
        P.wait("scalar", s_stat)
        P.op("scalar", lambda e, jn=jn: e.copy(
            srow[0:1, jn * 512:(jn + 1) * 512], pb[6][0:1, :]), inc=s_sc)
        sl_bank.consumed(6, s_sc)
    P.wait("vector", s_sc)
    for h in range(NH):
        P.op("vector", lambda e, h=h: e.reduce_sum(
            sc1[0:1, 16 + h:17 + h], srow[0:1, h * 256:(h + 1) * 256], AX),
            inc=s_vq if h == NH - 1 else None)
    P.barrier()

    P.op("vector", lambda e: e.memset(sc1[0:1, 11:12], 0.0), inc=s_vq)
    for j in range(TT):
        sl_bank.acquire("tensor", 0)
        for c in range(FC):
            mm(pb[0][0:64, :], wkvb[:, c, :], xn2[:, c, j * 512:(j + 1) * 512],
               c == 0, c == FC - 1, inc=s_mm if c == FC - 1 else None)
        P.wait("scalar", s_mm)
        P.op("scalar", lambda e, j=j: e.copy(cb[:, j, :], pb[0][0:64, :]),
             inc=s_sc)
        sl_sq.acquire("scalar", 0)
        P.op("scalar", lambda e: e.activation(
            sq[0:64, 0, :], pb[0][0:64, :], AF.Square), inc=s_sc)
        sl_bank.consumed(0, s_sc)
        sl_bank.acquire("tensor", 1)
        P.wait("tensor", s_sc)
        mm(pb[1][0:1, :], ones64[:, :], sq[0:64, 0, :], True, True, inc=s_stat)
        sl_sq.consumed(0, s_stat)
        P.wait("vector", s_stat)
        P.op("vector", lambda e, j=j: e.reduce_max(
            sc1[0:1, 24 + j:25 + j], pb[1][0:1, :], AX), inc=s_vq)
        sl_bank.consumed(1, s_vq)
    vfence()
    P.op("vector", lambda e: e.reduce_max(
        sc1[0:1, 11:12], sc1[0:1, 24:24 + TT], AX), inc=s_vq)
    vfence()
    P.barrier()
    if upto == "chead":
        finish(); return _replay()

    for h in range(NH):
        P.op("vector", lambda e: e.memset(sc1[0:1, 13:14], 0.0), inc=s_vq)
        sl_qb.acquire("vector", 0)
        pending_qn = [None]

        def run_pending_qn():
            if pending_qn[0] is not None:
                f = pending_qn[0]
                pending_qn[0] = None
                f()

        for j in range(TT):
            qnb = 2 + (j % 2)
            for dh in range(2):
                bslot = dh
                sl_bank.acquire("tensor", bslot)
                for c in range(FC):
                    mm(pb[bslot][:, :],
                       wqb[:, c, h * HD + dh * 128:h * HD + (dh + 1) * 128],
                       xn2[:, c, j * 512:(j + 1) * 512], c == 0, c == FC - 1,
                       inc=s_mm if c == FC - 1 else None)
                run_pending_qn()
                P.wait("vector", s_mm)
                P.op("vector", lambda e, dh=dh, j=j, b=bslot: e.tensor_copy(
                    qb[:, dh, j * 512:(j + 1) * 512], pb[b][:, :]), inc=s_ev)
                sl_bank.consumed(bslot, s_ev)
                if "noqn" in flags:
                    continue
                sqs = (2 * j + dh) % 4
                sl_sq.acquire("scalar", sqs)
                P.wait("scalar", s_ev)
                P.op("scalar", lambda e, j=j, dh=dh, sqs=sqs: e.activation(
                    sq[:, sqs, :], qb[:, dh, j * 512:(j + 1) * 512], AF.Square),
                    inc=s_sc)
                sq_done = s_sc.n

                def qn_closure(j=j, dh=dh, qnb=qnb, sqs=sqs, sq_done=sq_done):
                    if dh == 0:
                        sl_bank.acquire("tensor", qnb)
                    P.wait("tensor", s_sc, sq_done)
                    mm(pb[qnb][0:1, :], ones_b[:, 0:1], sq[:, sqs, :],
                       dh == 0, dh == 1, inc=s_stat if dh == 1 else None)
                    sl_sq.consumed(sqs, s_stat)
                    if dh == 1:
                        P.wait("vector", s_stat)
                        P.op("vector", lambda e, q=qnb, j=j: e.reduce_max(
                            sc1[0:1, 24 + j:25 + j], pb[q][0:1, :], AX),
                            inc=s_vq)
                        sl_bank.consumed(qnb, s_vq)
                pending_qn[0] = qn_closure
        run_pending_qn()
        vfence()
        P.op("vector", lambda e: e.reduce_max(
            sc1[0:1, 13:14], sc1[0:1, 24:24 + TT], AX), inc=s_vq)
        if upto == "qmm":
            break
        vfence()
        P.op("vector", lambda e, h=h: e.tensor_scalar(
            sc1[0:1, 14:15], sc1[0:1, 13:14], sc1[0:1, 16 + h:17 + h],
            sc1[0:1, 11:12], OP.mult, OP.mult), inc=s_vq)
        P.wait("scalar", s_vq)
        P.op("scalar", lambda e: e.activation(sc1[0:1, 15:16], sc1[0:1, 14:15],
                                              AF.Sqrt), inc=s_sc)
        P.wait("vector", s_sc)
        P.op("vector", lambda e: e.tensor_scalar_mul(
            sc1[0:1, 12:13], sc1[0:1, 15:16], -1.0 / 16.0), inc=s_vq)
        vfence()
        sl_bank.acquire("tensor", 4)
        P.wait("tensor", s_vq)
        mm(pb[4][:, 4:5], ones_r[:, :], sc1[0:1, 12:13], True, True, inc=s_bc)
        P.wait("scalar", s_bc)
        P.op("scalar", lambda e: e.copy(svec[:, 4:5], pb[4][:, 4:5]), inc=s_sc)
        sl_bank.consumed(4, s_sc)

        if upto == "qgen":
            break
        sl_kb.acquire("vector", 0)
        for dh in range(2):
            for j in range(TT):
                bslot = (dh * TT + j) % 4
                sl_bank.acquire("tensor", bslot)
                mm(pb[bslot][:, :],
                   wkub[:, h * HD + dh * 128:h * HD + (dh + 1) * 128],
                   cb[:, j, :], True, True, inc=s_mm)
                P.wait("vector", s_mm)
                P.op("vector", lambda e, dh=dh, j=j, b=bslot: e.tensor_copy(
                    kb[:, dh, j * 512:(j + 1) * 512], pb[b][:, :]), inc=s_ev)
                sl_bank.consumed(bslot, s_ev)
        sl_vb.acquire("vector", 0)
        for kc in range(KCN):
            bslot = kc % 4
            j, off = kc // 4, (kc % 4) * 128
            sl_bank.acquire("tensor", bslot)
            mm(pb[bslot][:, 0:256], cb[:, j, off:off + 128],
               wvub[:, h * HD:(h + 1) * HD], True, True, inc=s_mm)
            P.wait("vector", s_mm)
            P.op("vector", lambda e, kc=kc, b=bslot: e.tensor_copy(
                vb[:, kc, :], pb[b][:, 0:256]), inc=s_ev)
            sl_bank.consumed(bslot, s_ev)
        P.barrier(("tensor", "scalar", "vector", "sync"))
        if upto == "kv":
            break

        pend_tail = [None]

        def run_tail():
            if pend_tail[0] is not None:
                f = pend_tail[0]
                pend_tail[0] = None
                f()

        for jq in range(TT):
            qs = slice(jq * 512, (jq + 1) * 512)
            if jq == 0:
                P.wait("vector", s_stat)
                P.op("vector", lambda e: e.memset(dn[:, :], 0.0), inc=s_vq)
                P.wait("tensor", s_vq)

            def emit_scores(kc):
                sslot = kc % 4
                sl_bank.acquire("tensor", sslot)
                for dh in range(2):
                    mm(pb[sslot][:, :], kb[:, dh, kc * 128:(kc + 1) * 128],
                       qb[:, dh, qs], dh == 0, dh == 1,
                       inc=s_mm if dh == 1 else None)
                return s_mm.n

            LOOK = 3
            scores_done = {}
            scores_done[0] = emit_scores(0)
            run_tail()
            for k0 in range(1, min(LOOK, KCN)):
                scores_done[k0] = emit_scores(k0)
            for kc in range(KCN):
                sslot = kc % 4
                pslot = kc % 4
                if kc == 0:
                    sl_bank.acquire("tensor", 4)
                    sl_bank.acquire("tensor", 5)
                sl_pt.acquire("scalar", pslot)
                P.wait("scalar", s_mm, scores_done[kc])
                P.op("scalar", lambda e, ss=sslot, ps=pslot: e.activation(
                    pt[:, ps, :], pb[ss][:, :], AF.Exp,
                    bias=svec[:, 4:5], scale=1.0 / 16.0), inc=s_sc)
                exp_done = s_sc.n
                sl_bank.consumed(sslot, s_sc)
                if kc % 2 == 1:
                    P.wait("vector", s_sc, exp_done)
                    P.op("vector", lambda e, ps=pslot: e.tensor_add(
                        dn[:, :], dn[:, :], pt[:, ps - 1, :]))
                    P.op("vector", lambda e, ps=pslot: e.tensor_add(
                        dn[:, :], dn[:, :], pt[:, ps, :]), inc=s_vq)
                    sl_pt.consumed(pslot - 1, s_vq)
                    sl_pt.consumed(pslot, s_vq)
                if kc + LOOK < KCN:
                    scores_done[kc + LOOK] = emit_scores(kc + LOOK)
                P.wait("tensor", s_sc, exp_done)
                for dh in range(2):
                    mm(pb[4 + dh][:, :], vb[:, kc, dh * 128:(dh + 1) * 128],
                       pt[:, pslot, :], kc == 0, kc == KCN - 1,
                       inc=s_mm if dh == 1 else None)
                sl_pt.consumed(pslot, s_mm)

            def tail(jq=jq, h=h, last=(jq == TT - 1)):
                sl_bank.acquire("tensor", 6)
                P.wait("tensor", s_vq)
                mm(pb[6][0:1, :], ones_b[:, 0:1], dn[:, :], True, True,
                   inc=s_stat)
                if not last:
                    P.wait("vector", s_stat)
                    P.op("vector", lambda e: e.memset(dn[:, :], 0.0),
                         inc=s_vq)
                sl_srow.acquire("scalar", 0)
                P.wait("scalar", s_stat)
                P.op("scalar", lambda e: e.copy(srow[0:1, 0:512],
                                                pb[6][0:1, :]), inc=s_sc)
                sl_bank.consumed(6, s_sc)
                P.wait("vector", s_sc)
                P.op("vector", lambda e: e.reciprocal(
                    srow[0:1, 512:1024], srow[0:1, 0:512]), inc=s_vq)
                sl_bank.acquire("tensor", 7)
                P.wait("tensor", s_vq)
                mm(pb[7][:, :], ones_r[:, :], srow[0:1, 512:1024], True, True,
                   inc=s_bc)
                sl_srow.consumed(0, s_bc)
                P.wait("vector", s_bc)
                P.op("vector", lambda e: e.tensor_copy(rdnB[:, :],
                                                       pb[7][:, :]), inc=s_vq)
                sl_bank.consumed(7, s_vq)
                P.wait("vector", s_mm)
                for dh in range(2):
                    ob = jq % 2
                    sl_obuf.acquire("vector", ob)
                    P.op("vector", lambda e, dh=dh, ob=ob: e.tensor_mul(
                        obuf[:, ob, dh, :], pb[4 + dh][:, :], rdnB[:, :]),
                        inc=s_ev)
                    sl_bank.consumed(4 + dh, s_ev)
                    P.wait("sync", s_ev)
                    P.dma("sync", oh[2 * h + dh, jq], obuf[:, ob, dh, :],
                          d_ob[ob])
                    sl_obuf.consumed(ob, d_ob[ob])
            pend_tail[0] = tail
        run_tail()
        sl_qb.consumed(0, s_mm)
        sl_kb.consumed(0, s_mm)
        sl_vb.consumed(0, s_mm)
        if upto in ("head0", "qgen", "kv", "qmm"):
            break

    P.barrier()
    if upto in ("attn", "head0", "qgen", "kv", "qmm"):
        finish(); return _replay()

    sl_wo = Slots(P, 2)
    for cc2 in range(16):
        sl_wo.acquire("gpsimd", cc2 % 2)
        P.dma("gpsimd", wsto[:, cc2 % 2, 0:1024],
              woT[cc2 * 128:(cc2 + 1) * 128, :], d_wo[cc2 % 2])
        P.wait("vector", d_wo[cc2 % 2])
        P.op("vector", lambda e, cc2=cc2: e.tensor_copy(
            wob[:, cc2, :], wsto[:, cc2 % 2, 0:1024]), inc=s_vq)
        sl_wo.consumed(cc2 % 2, s_vq)
    P.op("vector", lambda e: e.memset(smax[:, 4:5], 0.0), inc=s_vq)
    P.barrier()
    pending_wo = [None]

    def run_pending_wo():
        if pending_wo[0] is not None:
            f = pending_wo[0]
            pending_wo[0] = None
            f()

    for j in range(TT):
        buf = j % 2
        sl_ot.acquire("gpsimd", buf)
        for cc2 in range(16):
            P.dma("gpsimd", ot[:, buf, cc2, :], oh[cc2, j], d_ot[buf])
        load_rtile(r1h, j, buf)
        P.wait("tensor", d_ot[buf])
        P.wait("tensor", d_rt[buf])
        for oc in range(HC):
            bslot = oc % 4
            sl_bank.acquire("tensor", bslot)
            for cc2 in range(16):
                mm(pb[bslot][:, :], wob[:, cc2, oc * 128:(oc + 1) * 128],
                   ot[:, buf, cc2, :], cc2 == 0, cc2 == 15,
                   inc=s_mm if cc2 == 15 else None)
            if oc == 0:
                run_pending_wo()
            P.wait("vector", s_mm)
            P.op("vector", lambda e, oc=oc, buf=buf, b=bslot: e.tensor_add(
                rt[:, buf, oc, :], pb[b][:, :], rt[:, buf, oc, :]), inc=s_ev)
            sl_bank.consumed(bslot, s_ev)
            P.wait("sync", s_ev)
            P.dma("sync", r2h[oc * 128:(oc + 1) * 128, j * 512:(j + 1) * 512],
                  rt[:, buf, oc, :], d_rts[buf])
            if "tap_r2" in tapd:
                P.dma("sync", tapd["tap_r2"][oc * 128:(oc + 1) * 128,
                                             j * 512:(j + 1) * 512],
                      rt[:, buf, oc, :], d_rts[buf])
        sl_ot.consumed(buf, s_mm)

        def wo_stats(j=j, buf=buf):
            P.wait("scalar", s_ev)
            P.wait("vector", s_ev)
            ln_stats_tile(rt[:, buf], j, statsh3)
            bcast_pair(srow[0:1, 1024:1536], srow[0:1, 1536:2048], 6, 7, s_row)
            P.wait("vector", s_bc)
            for c in range(FC):
                cs = c % 2
                sl_ut.acquire("vector", cs)
                sl_ut.acquire("vector", 2 + cs)
                P.op("vector", lambda e, c=c, buf=buf, cs=cs: e.tensor_mul(
                    u_t[:, cs, :], rt[:, buf, c, :], pb[6][:, :]))
                P.op("vector", lambda e, cs=cs: e.tensor_sub(
                    u_t[:, 2 + cs, :], u_t[:, cs, :], pb[7][:, :]))
                P.op("vector", lambda e, c=c, cs=cs: e.reduce_max(
                    smax[:, 8 + c:9 + c], u_t[:, 2 + cs, :], AX,
                    apply_absolute_value=True), inc=s_vq)
                sl_ut.consumed(cs, s_vq)
                sl_ut.consumed(2 + cs, s_vq)
            vfence()
            P.op("vector", lambda e, j=j: e.reduce_max(
                smax[:, 16 + j:17 + j], smax[:, 8:16], AX), inc=s_vq)
            sl_rt.consumed(buf, s_vq)
            sl_bank.consumed(6, s_vq)
            sl_bank.consumed(7, s_vq)
        pending_wo[0] = wo_stats
        sl_rt.consumed(buf, d_rts[buf])
    P.wait("gpsimd", s_mm)
    for c in range(FC):
        P.dma("gpsimd", wgc[:, c, :], wgc_h[c], s_ld)
        P.dma("gpsimd", wuc[:, c, :], wuc_h[c], s_ld)
    run_pending_wo()
    vfence()
    P.op("vector", lambda e: e.reduce_max(
        smax[:, 4:5], smax[:, 16:16 + TT], AX), inc=s_vq)
    vfence()
    P.op("vector", lambda e: e.tensor_copy(smax[:, 0:1], smax[:, 4:5]), inc=s_vq)
    ffn_passB(r2h, statsh3, cc_idx=2)
    ffn_passC(r2h, out_h, cc_idx=3, store_sem=None)
    finish()
    return _replay()


def kernel(L0_attn_x, L0_attn_Wq, L0_attn_Wkv_down, L0_attn_Wk_up, L0_attn_Wv_up,
           L0_attn_Wo, L0_ffn_clip_x, L0_ffn_clip_down, L0_ffn_w_gate,
           L0_ffn_w_up, L0_ffn_w_down):
    x = np.asarray(L0_attn_x, dtype=np.float32)
    B, T, _ = x.shape
    c = np.ascontiguousarray
    w = {
        "wqT": c(np.asarray(L0_attn_Wq, np.float32).T),
        "wkvT": c(np.asarray(L0_attn_Wkv_down, np.float32).T),
        "wkuT": c(np.asarray(L0_attn_Wk_up, np.float32).T),
        "wvuT": c(np.asarray(L0_attn_Wv_up, np.float32).T),
        "woT": c(np.asarray(L0_attn_Wo, np.float32).T),
        "wg_rm": c(np.asarray(L0_ffn_w_gate, np.float32)),
        "wu_rm": c(np.asarray(L0_ffn_w_up, np.float32)),
        "wd_rm": c(np.asarray(L0_ffn_w_down, np.float32)),
        "clipx": np.asarray(L0_ffn_clip_x, np.float32).reshape(1, 1),
        "clipd": np.asarray(L0_ffn_clip_down, np.float32).reshape(1, 1),
        "ident": np.eye(128, dtype=np.float32).astype(ml_dtypes.bfloat16),
    }
    nc = build_nc(T=T)
    in_maps = [dict(w, xT=c(x[b].T)) for b in range(B)]
    res = run_bass_kernel_spmd(nc, in_maps, core_ids=list(range(B)))
    out = np.stack([np.asarray(r["out"]).T for r in res.results], axis=0)
    return np.ascontiguousarray(out.astype(np.float32))
